# version 01
# baseline (speedup 1.0000x reference)
import numpy as np
import ml_dtypes

import concourse.bass as bass
import concourse.mybir as mybir
from concourse.bass_utils import run_bass_kernel_spmd
from concourse.bass_isa import ReduceOp

F32 = mybir.dt.float32
BF16 = mybir.dt.bfloat16
FP8 = mybir.dt.float8e4
AF = mybir.ActivationFunctionType
ALU = mybir.AluOpType
AX = mybir.AxisListType

DM = 2048
TLOC = 1024
TX = TLOC + 3
HK, HV, DK, DV = 16, 32, 128, 128
NCH, CL = 8, 128
SOLVE_LEVELS = 4
DFF = 8192

ENGINES = ("tensor", "vector", "scalar", "sync", "gpsimd")


def vr(v):
    return (v % 2) * 16 + v // 2


class Res:
    __slots__ = ("name", "writer", "readers")

    def __init__(self, name):
        self.name = name
        self.writer = None
        self.readers = []


class T:
    __slots__ = ("ap", "res")

    def __init__(self, ap, res):
        self.ap = ap
        self.res = res

    def __getitem__(self, idx):
        return T(self.ap[idx], self.res)


class B:
    def __init__(self, dbg=()):
        self.nc = bass.Bass("TRN2", target_bir_lowering=False, num_devices=8,
                            detect_race_conditions=False)
        self.ops = {e: [] for e in ENGINES}
        self.waited = {e: {} for e in ENGINES}
        self.opidx = {e: 0 for e in ENGINES}
        self.last_drain = {e: -1 for e in ENGINES}
        self.dbg = set(dbg)
        self.dbg_outs = {}
        nc = self.nc
        self.sems = {}
        for e in ENGINES:
            self.sems[e] = Ctr(nc.alloc_semaphore("sem_" + e))
        self.dpool = {q: [Ctr(nc.alloc_semaphore(f"dq_{q}_{i}")) for i in range(32)]
                      for q in ("sync", "gpsimd")}
        self.dcnt = {"sync": 0, "gpsimd": 0}
        self.dset = set()
        for q in self.dpool:
            self.dset.update(self.dpool[q])

        self._declare_dram()
        self._declare_consts()
        self._declare_sbuf()

    def on(self, eng, fn):
        self.ops[eng].append(fn)

    def _wait(self, eng, sem, val):
        if val <= 0:
            return
        if self.waited[eng].get(sem, -1) >= val:
            return
        self.waited[eng][sem] = val
        self.on(eng, lambda e, s=sem.h, v=val: e.wait_ge(s, v))

    def emit(self, eng, fn, reads=(), writes=(), dma=None):
        rres = [x.res if isinstance(x, T) else x for x in reads]
        wres = [x.res if isinstance(x, T) else x for x in writes]
        need_drain = False
        for r in rres + wres:
            w = r.writer
            if w is None:
                continue
            weng, wsem, wval, widx = w
            if wsem in self.dset or weng != eng:
                self._wait(eng, wsem, wval)
            elif widx > self.last_drain[eng]:
                need_drain = True
        for r in wres:
            for (reng, rsem, rval, ridx) in r.readers:
                if rsem in self.dset or reng != eng:
                    self._wait(eng, rsem, rval)
                elif ridx > self.last_drain[eng]:
                    need_drain = True
        if need_drain and eng in ("vector", "scalar", "gpsimd"):
            self.on(eng, lambda e: e.drain())
            self.last_drain[eng] = self.opidx[eng]
        if dma is None:
            sem = self.sems[eng]
            sem.n += 1
            val = sem.n
            self.on(eng, lambda e, fn=fn, s=sem.h: fn(e).then_inc(s, 1))
        else:
            pool = self.dpool[dma]
            sem = pool[self.dcnt[dma] % len(pool)]
            self.dcnt[dma] += 1
            sem.n += 16
            val = sem.n
            self.on(eng, lambda e, fn=fn, s=sem.h: fn(e).then_inc(s, 16))
        self.opidx[eng] += 1
        idx = self.opidx[eng]
        for r in rres:
            for j, (re_, rs_, rv_, ri_) in enumerate(r.readers):
                if re_ == eng and rs_ is sem:
                    r.readers[j] = (eng, sem, val, idx)
                    break
            else:
                r.readers.append((eng, sem, val, idx))
        for r in wres:
            r.writer = (eng, sem, val, idx)
            r.readers = []
        return (sem, val)

    def barrier_on(self, eng, marks):
        for (sem, val) in marks:
            self._wait(eng, sem, val)

    def fence(self):
        allsems = list(self.sems.values()) + [c for q in self.dpool for c in self.dpool[q]]
        for eng in ENGINES:
            for sem in allsems:
                if sem is self.sems.get(eng):
                    continue
                self._wait(eng, sem, sem.n)

    def mm(self, out, lhsT, rhs, start=True, stop=True, reads=(), writes=()):
        o = out.ap if isinstance(out, T) else out
        l = lhsT.ap if isinstance(lhsT, T) else lhsT
        r = rhs.ap if isinstance(rhs, T) else rhs
        rd = [x for x in (lhsT, rhs) if isinstance(x, T)] + list(reads)
        wr = ([out] if isinstance(out, T) else []) + list(writes)
        return self.emit("tensor", lambda t: t.matmul(o, l, r, start=start, stop=stop),
                         reads=rd, writes=wr)

    def tr(self, out, in_, f32=False, p=128):
        o = out.ap if isinstance(out, T) else out
        i = in_.ap if isinstance(in_, T) else in_
        ident = self.ident_f32s if f32 else self.ident_bfs
        iap = ident.ap if p == 128 else ident.ap[0:p, 0:p]
        rd = [x for x in (in_,) if isinstance(x, T)]
        wr = [out] if isinstance(out, T) else []
        return self.emit("tensor", lambda t: t.transpose(o, i, iap),
                         reads=rd + [ident], writes=wr)

    def vop(self, fn, reads=(), writes=()):
        return self.emit("vector", fn, reads=reads, writes=writes)

    def sop(self, fn, reads=(), writes=()):
        return self.emit("scalar", fn, reads=reads, writes=writes)

    def gop(self, fn, reads=(), writes=()):
        return self.emit("gpsimd", fn, reads=reads, writes=writes)

    def dma(self, queue, out, in_):
        o = out.ap if isinstance(out, T) else out
        i = in_.ap if isinstance(in_, T) else in_
        rd = [x for x in (in_,) if isinstance(x, T)]
        wr = [out] if isinstance(out, T) else []
        return self.emit(queue, lambda e: e.dma_start(out=o, in_=i),
                         reads=rd, writes=wr, dma=queue)

    def act(self, out, in_, func, bias=0.0, scale=1.0, accum_out=None):
        o = out.ap if isinstance(out, T) else out
        i = in_.ap if isinstance(in_, T) else in_
        ao = accum_out.ap if isinstance(accum_out, T) else accum_out
        wr = [x for x in (out, accum_out) if isinstance(x, T)]
        rd = [x for x in (in_, bias, scale) if isinstance(x, T)]
        bias = bias.ap if isinstance(bias, T) else bias
        scale = scale.ap if isinstance(scale, T) else scale
        if ao is None:
            return self.sop(lambda e: e.activation(out=o, in_=i, func=func, bias=bias, scale=scale),
                            reads=rd, writes=wr)
        return self.sop(lambda e: e.activation(out=o, in_=i, func=func, bias=bias, scale=scale,
                                               accum_out=ao), reads=rd, writes=wr)

    def tt(self, out, in0, in1, op=ALU.mult):
        o, a, c = (x.ap if isinstance(x, T) else x for x in (out, in0, in1))
        rd = [x for x in (in0, in1) if isinstance(x, T)]
        wr = [out] if isinstance(out, T) else []
        return self.vop(lambda e: e.tensor_tensor(out=o, in0=a, in1=c, op=op), reads=rd, writes=wr)

    def ts(self, out, in0, s1, s2=None, op0=ALU.mult, op1=ALU.bypass):
        o, a = (x.ap if isinstance(x, T) else x for x in (out, in0))
        s1a = s1.ap if isinstance(s1, T) else s1
        s2a = s2.ap if isinstance(s2, T) else s2
        rd = [x for x in (in0, s1, s2) if isinstance(x, T)]
        wr = [out] if isinstance(out, T) else []
        if s2a is None and op1 == ALU.bypass:
            return self.vop(lambda e: e.tensor_single_scalar(out=o, in_=a, scalar=s1a, op=op0),
                            reads=rd, writes=wr)
        return self.vop(lambda e: e.tensor_scalar(out=o, in0=a, scalar1=s1a, scalar2=s2a,
                                                  op0=op0, op1=op1), reads=rd, writes=wr)

    def stt(self, out, in0, scalar, in1, op0=ALU.mult, op1=ALU.add):
        o, a, c = (x.ap if isinstance(x, T) else x for x in (out, in0, in1))
        sa = scalar.ap if isinstance(scalar, T) else scalar
        rd = [x for x in (in0, scalar, in1) if isinstance(x, T)]
        wr = [out] if isinstance(out, T) else []
        return self.vop(lambda e: e.scalar_tensor_tensor(out=o, in0=a, scalar=sa, in1=c,
                                                         op0=op0, op1=op1), reads=rd, writes=wr)

    def cp(self, out, in_, eng="vector"):
        o, a = (x.ap if isinstance(x, T) else x for x in (out, in_))
        rd = [x for x in (in_,) if isinstance(x, T)]
        wr = [out] if isinstance(out, T) else []
        if eng == "scalar":
            return self.sop(lambda e: e.copy(out=o, in_=a), reads=rd, writes=wr)
        return self.vop(lambda e: e.tensor_copy(out=o, in_=a), reads=rd, writes=wr)

    def _declare_dram(self):
        nc = self.nc

        def din(name, shape, dt=F32):
            return T(nc.dram_tensor(name, list(shape), dt, kind="ExternalInput").ap(),
                     Res(name))

        self.x_in = din("x_in", [DM, TX])
        self.w_qkvz = din("w_qkvz", [DM, 12288])
        self.w_ba = din("w_ba", [DM, 64])
        self.conv_w = din("conv_w", [8192, 4])
        self.dt_bias = din("dt_bias", [HV, 1])
        self.a_log = din("a_log", [HV, 1])
        self.norm_w = din("norm_w", [DV, 1])
        self.w_out = din("w_out", [HV * DV, DM])
        self.time_decay = din("time_decay", [128, 16])
        self.time_first = din("time_first", [128, 16])
        self.w_k = din("w_k", [DM, DM])
        self.w_v = din("w_v", [DM, DM])
        self.w_r = din("w_r", [DM, DM])
        self.w_o = din("w_o", [DM, DM])
        self.w_fk = din("w_fk", [DM, DFF])
        self.w_fv = din("w_fv", [DFF, DM])
        self.ln1_w = din("ln1_w", [128, 16])
        self.ln1_b = din("ln1_b", [128, 16])
        self.ln2_w = din("ln2_w", [128, 16])
        self.ln2_b = din("ln2_b", [128, 16])
        self.w_ffn = din("w_ffn", [DM, DM])
        self.half_mask = din("half_mask", [128, 1])
        self.first_mask = din("first_mask", [128, 1])
        self.x_out = T(nc.dram_tensor("x_out", [DM, TLOC], F32, kind="ExternalOutput").ap(),
                       Res("x_out"))
        self.RG = [[0, 1], [2, 3], [4, 5], [6, 7]]
        mkint = lambda n, s, d: T(nc.dram_tensor(n, list(s), d, kind="Internal").ap(), Res(n))
        self.agA_in = mkint("agA_in", [HV * DK, DV], BF16)
        self.agA_out = mkint("agA_out", [2 * HV * DK, DV], BF16)
        self.agB_in = mkint("agB_in", [128, 16], F32)
        self.agB_out = mkint("agB_out", [256, 16], F32)
        self.spill = [mkint(f"spill{i}", [128, 4 * 128], BF16) for i in range(HV * NCH)]
        self.y_sp = [mkint(f"y_sp{i}", [128, TLOC], BF16) for i in range(HK * 4)]
        self.z_sp = [mkint(f"z_sp{v}", [128, TLOC], BF16) for v in range(HV)]
        self.rwkv_sp = [mkint(f"rwkv_sp{i}", [128, TLOC], BF16) for i in range(16)]
        self.xres_sp = [mkint(f"xres_sp{i}", [128, TLOC], F32) for i in range(16)]
        self.gated_sp = [mkint(f"gated_sp{v}", [128, TLOC], BF16) for v in range(HV)]
        self.wkv_sp = [mkint(f"wkv_sp{i}", [128, TLOC], BF16) for i in range(16)]
        self.h2_sp = [mkint(f"h2_sp{i}", [128, TLOC], BF16) for i in range(DFF // 128)]
        self.wd_sp = mkint("wd_sp", [DM, DM], BF16)

    def _declare_consts(self):
        nc = self.nc
        it = nc.inline_tensor
        self._const_specs = []

        def const(name, arr, dt):
            dram = it(np.ascontiguousarray(arr), name="cd_" + name).ap()
            sb = nc.alloc_sbuf_tensor("c_" + name, list(arr.shape), dt)
            t = T(sb[:, :] if len(arr.shape) == 2 else sb[:], Res("c_" + name))
            self._const_specs.append((t, dram))
            return t

        bf = ml_dtypes.bfloat16
        self.ident_bfs = const("ident_bf", np.eye(128, dtype=bf), BF16)
        self.ident_f32s = const("ident_f32", np.eye(128, dtype=np.float32), F32)
        self.ones_col_bfs = const("ones_col_bf", np.ones((128, 1), dtype=bf), BF16)
        self.ones_row_f32s = const("ones_row_f32", np.ones((1, 128), dtype=np.float32), F32)
        self.ones_scan = const("ones_scan", np.ones((32, TLOC), dtype=np.float32), F32)
        ms = np.where(np.arange(128)[:, None] < np.arange(128)[None, :], 0.0, -1e6)
        mi = np.where(np.arange(128)[:, None] <= np.arange(128)[None, :], 0.0, -1e6)
        self.maskb_strict = const("maskb_s", ms.astype(bf), BF16)
        self.maskb_incl = const("maskb_i", mi.astype(bf), BF16)
        sel = np.zeros((32, 32 * 128), dtype=bf)
        for r in range(32):
            sel[r, r * 128:(r + 1) * 128] = 1.0
        self.selb32 = const("selb32", sel, BF16)
        eps = np.zeros((128, 6), dtype=np.float32)
        eps[:, 0] = 1e-5
        eps[:, 1] = 1e-6
        eps[:, 2] = 1.0
        eps[:, 3] = 1.0 / 128.0
        eps[:, 4] = -32.0 * float(np.log(2.0))
        self.epsc = const("epsc", eps, F32)
        self.selb16 = T(self.selb32.ap, self.selb32.res)

    def load_consts(self):
        for (t, dram) in self._const_specs:
            self.dma("sync", t, dram)

    def _declare_sbuf(self):
        nc = self.nc
        self._tcount = 0

        def sbuf(name, shape, dt):
            ap = nc.alloc_sbuf_tensor(name, list(shape), dt)
            return T(ap[:, :], Res(name))

        self.sbuf = sbuf
        self.X = [sbuf(f"X{i}", [128, TX], F32) for i in range(16)]
        self.POOLT = nc.alloc_sbuf_tensor("POOL", [128, 27648], BF16)
        self._pool_res = {}
        self.PF32 = [sbuf(f"PF32_{j}", [128, 1056], F32) for j in range(3)]
        self.WBUF = [sbuf(f"WBUF{j}", [128, 2048], BF16) for j in range(2)]
        self.BAB = sbuf("BAB", [32, TX], F32)
        self.BAA = sbuf("BAA", [32, TX], F32)
        self.G32 = T(self.PF32[0].ap[0:32, 0:TLOC], self.PF32[0].res)
        self.GLNB = T(self.PF32[2].ap[0:32, 0:TLOC], self.PF32[2].res)
        self.SCR32 = T(self.PF32[1].ap[0:32, 0:TLOC], self.PF32[1].res)
        self.SQQ2 = sbuf("SQQ2", [16, TLOC], F32)
        self.SQK2 = sbuf("SQK2", [16, TLOC], F32)
        self.REBA = sbuf("REBA", [32, CL], F32)
        self.REBB = sbuf("REBB", [32, CL], F32)
        self.REBC = sbuf("REBC", [32, CL], F32)
        self.STK = sbuf("STK", [128, CL], F32)
        self.COLS = sbuf("COLS", [128, CL], F32)
        self.EARG0 = sbuf("EARG0", [128, 128], F32)
        self.ECOL = sbuf("ECOL", [128, 128], F32)
        self.CTRIG = [sbuf(f"CTRIG{j}", [32, CL], BF16) for j in range(3)]
        self.CTRIB = [sbuf(f"CTRIB{j}", [32, CL], BF16) for j in range(3)]
        self.CTRI2Q = [sbuf(f"CTRI2Q_{j}", [16, CL], BF16) for j in range(3)]
        self.CTRI2K = [sbuf(f"CTRI2K_{j}", [16, CL], BF16) for j in range(3)]
        self.TRIF = [sbuf(f"TRIF{j}", [32, CL], F32) for j in range(2)]
        self.TRIF2 = [sbuf(f"TRIF2_{j}", [32, CL], F32) for j in range(2)]
        self.KTM = [sbuf("KTM0", [128, 128], BF16)] * 2
        self.VTM = [sbuf("VTM0", [128, 256], BF16)] * 2
        self.KNF = [sbuf("KNF0", [128, 128], BF16)] * 2
        self.QNF = [sbuf("QNF0", [128, 128], BF16)] * 2
        self.QHF = [sbuf(f"QHF{j}", [128, 128], BF16) for j in range(2)]
        self.KKT = [sbuf("KKT0", [128, 128], BF16)] * 2
        self.KQT = [sbuf("KQT0", [128, 128], BF16)] * 2
        self.EB = [sbuf(f"EB{j}", [128, 128], BF16) for j in range(2)] * 2
        self.YLD = [sbuf("YLD0", [128, 512], BF16)] * 2
        self.ARG = [sbuf(f"ARG{j}", [128, 128], F32) for j in range(2)]
        self.EARGB = [sbuf(f"EARGB{j}", [128, 128], BF16) for j in range(2)]
        self.MTP = [sbuf(f"MTP{j}", [128, 128], BF16) for j in range(4)]
        self.MXP = [sbuf(f"MXP{j}", [128, 128], BF16) for j in range(2)]
        self.WOTT = [sbuf(f"WOTT{j}", [128, 128], BF16) for j in range(2)]
        self.RSOL = [sbuf(f"RSOL{j}", [128, 256], BF16) for j in range(4)]
        self.BKD = [sbuf(f"BKD{j}", [128, 128], BF16) for j in range(2)]
        self.BVK = [sbuf(f"BVK{j}", [128, 256], BF16) for j in range(2)]
        self.EFUW = [sbuf(f"EFUW{j}", [128, 512], BF16) for j in range(2)]
        self.DCREP = sbuf("DCREP", [128, HV * NCH], F32)
        self.DCROW = sbuf("DCROW", [32, NCH], F32)
        self.DCROWB = sbuf("DCROWB", [32, NCH], BF16)
        self.STAT = T(self.PF32[0].ap[0:1, :], self.PF32[0].res)
        self.ROW = T(self.PF32[1].ap[0:1, :], self.PF32[1].res)
        self.ROW2 = T(self.PF32[2].ap[0:1, :], self.PF32[2].res)
        self.RSTDB = sbuf("RSTDB", [128, TX], F32)
        self.NMRB = sbuf("NMRB", [128, TX], F32)
        self.CVEC = sbuf("CVEC", [128, 24], F32)
        self.WCOL = sbuf("WCOL", [128, 4], F32)
        self.DIAG = [sbuf(f"DIAG{j}", [128, 128], BF16) for j in range(4)]
        self.XQ8 = [sbuf("XQ8_0", [128, 1024], FP8)] * 2
        self.CVECB = sbuf("CVECB", [128, 128], F32)
        self.HR2 = [sbuf(f"HR2_{j}", [128, 512], BF16) for j in range(2)]
        self.VC0 = sbuf("VC0", [128, 16], F32)
        self.AGST = sbuf("AGST", [128, 16], F32)
        self.PSB = [nc.alloc_psum_tensor(f"PS{i}", [128, 512], F32) for i in range(7)]
        self.PBFB = nc.alloc_psum_tensor("PBF", [128, 1024], BF16)
        self.PRES = [Res(f"psbank{i}") for i in range(8)]
        self.PQ = []
        for i in range(7):
            for q in range(4):
                self.PQ.append(T(self.PSB[i][:, q * 128:(q + 1) * 128], self.PRES[i]))
        self.PH = [T(self.PSB[i][:, 0:256], self.PRES[i]) for i in range(7)] + \
                  [T(self.PSB[i][:, 256:512], self.PRES[i]) for i in range(7)]
        self.PFUL = [T(self.PSB[i][:, 0:512], self.PRES[i]) for i in range(7)]
        self.PB8 = [T(self.PBFB[:, j * 128:(j + 1) * 128], self.PRES[7]) for j in range(8)]

    def pb(self, off, n, tag=None):
        key = (off, n) if tag is None else tag
        if key not in self._pool_res:
            self._pool_res[key] = Res(f"pool@{off}+{n}")
        return T(self.POOLT[:, off:off + n], self._pool_res[key])

    def dbg_tap(self, tag, shape, dt=F32):
        if tag in self.dbg:
            t = T(self.nc.dram_tensor("dbg_" + tag, list(shape), dt,
                                      kind="ExternalOutput").ap(), Res("dbg_" + tag))
            self.dbg_outs[tag] = t
            return t
        return None

    def finalize(self):
        nc = self.nc
        with nc.Block() as block:
            @block.tensor
            def _(t):
                for fn in self.ops["tensor"]:
                    fn(t)

            @block.vector
            def _(e):
                for fn in self.ops["vector"]:
                    fn(e)

            @block.scalar
            def _(e):
                for fn in self.ops["scalar"]:
                    fn(e)

            @block.sync
            def _(e):
                for fn in self.ops["sync"]:
                    fn(e)

            @block.gpsimd
            def _(e):
                for fn in self.ops["gpsimd"]:
                    fn(e)
        return nc


class Ctr:
    def __init__(self, h):
        self.h = h
        self.n = 0



XLN_OFF = 0
MIX_OFF = 16896
Y_OFF = 21120
ZR_OFF = 25344


def rr(t, pat, **kw):
    return T(t.ap.rearrange(pat, **kw), t.res)


def sel16(b, h):
    return T(b.selb32.ap[0:16, h * 128:(h + 1) * 128], b.selb32.res)


def emit_ln(b, c0=0, ncols=TX):
    XB = [b.pb(MIX_OFF + j * 1056, 1056, tag=("mix", j)) for j in range(2)]
    SQ = [b.pb(MIX_OFF + (2 + j) * 1056, 1056, tag=("mix", 2 + j)) for j in range(2)]
    nch = [(0, 512), (512, 512)]
    if ncols > 1024:
        nch.append((1024, ncols - 1024))
    for i in range(16):
        b.cp(XB[i % 2][:, 0:ncols], b.X[i][:, c0:c0 + ncols])
        b.act(SQ[i % 2][:, 0:ncols], b.X[i][:, c0:c0 + ncols], AF.Square)
        for ci, (c0_, w) in enumerate(nch):
            b.mm(T(b.PSB[ci][0:1, 0:w], b.PFUL[ci].res), b.ones_col_bfs,
                 XB[i % 2][:, c0_:c0_ + w], start=(i == 0), stop=(i == 15))
            b.mm(T(b.PSB[3 + ci][0:1, 0:w], b.PFUL[3 + ci].res), b.ones_col_bfs,
                 SQ[i % 2][:, c0_:c0_ + w], start=(i == 0), stop=(i == 15))
    for ci, (c0_, w) in enumerate(nch):
        b.cp(b.STAT[0:1, c0_:c0_ + w], T(b.PSB[ci][0:1, 0:w], b.PFUL[ci].res))
        b.cp(b.ROW2[0:1, c0_:c0_ + w], T(b.PSB[3 + ci][0:1, 0:w], b.PFUL[3 + ci].res))
    b.ts(b.STAT[0:1, 0:ncols], b.STAT[0:1, 0:ncols], 1.0 / DM)
    b.ts(b.ROW2[0:1, 0:ncols], b.ROW2[0:1, 0:ncols], 1.0 / DM)
    b.act(b.ROW[0:1, 0:ncols], b.STAT[0:1, 0:ncols], AF.Square)
    b.tt(b.ROW2[0:1, 0:ncols], b.ROW2[0:1, 0:ncols], b.ROW[0:1, 0:ncols], op=ALU.subtract)
    b.act(b.ROW[0:1, 0:ncols], b.ROW2[0:1, 0:ncols], AF.Ln,
          bias=T(b.epsc.ap[0:1, 0:1], b.epsc.res))
    b.act(b.ROW2[0:1, 0:ncols], b.ROW[0:1, 0:ncols], AF.Exp, scale=-0.5)
    b.tt(b.STAT[0:1, 0:ncols], b.STAT[0:1, 0:ncols], b.ROW2[0:1, 0:ncols])
    b.ts(b.STAT[0:1, 0:ncols], b.STAT[0:1, 0:ncols], -1.0)
    for ci, (c0_, w) in enumerate(nch):
        b.mm(T(b.PSB[0][:, 0:w], b.PFUL[0].res), b.ones_row_f32s,
             b.ROW2[0:1, c0_:c0_ + w])
        b.mm(T(b.PSB[1][:, 0:w], b.PFUL[1].res), b.ones_row_f32s,
             b.STAT[0:1, c0_:c0_ + w])
        b.cp(b.RSTDB[:, c0_:c0_ + w], T(b.PSB[0][:, 0:w], b.PFUL[0].res))
        b.cp(b.NMRB[:, c0_:c0_ + w], T(b.PSB[1][:, 0:w], b.PFUL[1].res))


def emit_xln(b, c0=0, ncols=TX, aff=None):
    XLN = [b.pb(XLN_OFF + i * 1056, 1056, tag=("xln", i)) for i in range(16)]
    ch = [(0, min(1024, ncols))]
    if ncols > 1024:
        ch.append((1024, ncols - 1024))
    for i in range(16):
        xf = T(b.PF32[i % 2].ap[:, 0:1024], b.PF32[i % 2].res)
        for (q0, qw) in ch:
            b.tt(xf[:, 0:qw], b.X[i][:, c0 + q0:c0 + q0 + qw], b.RSTDB[:, q0:q0 + qw])
            if aff is None:
                b.tt(XLN[i][:, q0:q0 + qw], xf[:, 0:qw], b.NMRB[:, q0:q0 + qw], op=ALU.add)
            else:
                wcols, bcols = aff
                b.tt(xf[:, 0:qw], xf[:, 0:qw], b.NMRB[:, q0:q0 + qw], op=ALU.add)
                b.ts(XLN[i][:, q0:q0 + qw], xf[:, 0:qw],
                     T(wcols.ap[:, i:i + 1], wcols.res),
                     T(bcols.ap[:, i:i + 1], bcols.res), op0=ALU.mult, op1=ALU.add)
    return XLN


def emit_attn_a(b):
    emit_ln(b, c0=0, ncols=TX)
    dbgS = b.dbg_tap("lnstats", [128, 2 * TX], F32)
    if dbgS is not None:
        b.dma("sync", dbgS[:, 0:TX], b.RSTDB[:, 0:TX])
        b.dma("sync", dbgS[:, TX:2 * TX], b.NMRB[:, 0:TX])
    XLN = emit_xln(b, c0=0, ncols=TX)
    MIXT = [b.pb(MIX_OFF + j * 1056, 1056, tag=("mix", j)) for j in range(4)]
    YT = [b.pb(Y_OFF + j * 1056, 1056, tag=("yt", j)) for j in range(4)]
    ZROT = [b.pb(ZR_OFF + j * 1056, 1056, tag=("zr", j)) for j in range(2)]

    b.dma("gpsimd", rr(b.WBUF[0][:, 0:1024], "p (k o) -> p k o", k=16),
          rr(b.w_ba, "(k p) o -> p k o", p=128))
    for ci, (c0, w) in enumerate([(0, 512), (512, 512), (1024, 3)]):
        ptb = T(b.PSB[4][0:32, 0:w], b.PFUL[4].res)
        pta = T(b.PSB[5][0:32, 0:w], b.PFUL[5].res)
        for kt in range(16):
            b.mm(ptb, b.WBUF[0][:, kt * 64:kt * 64 + 32], XLN[kt][:, c0:c0 + w],
                 start=(kt == 0), stop=(kt == 15))
            b.mm(pta, b.WBUF[0][:, kt * 64 + 32:kt * 64 + 64], XLN[kt][:, c0:c0 + w],
                 start=(kt == 0), stop=(kt == 15))
        b.cp(T(b.BAB.ap[:, c0:c0 + w], b.BAB.res), ptb)
        b.cp(T(b.BAA.ap[:, c0:c0 + w], b.BAA.res), pta)

    dbgB = b.dbg_tap("bab", [64, TX], F32)
    if dbgB is not None:
        b.dma("sync", dbgB[0:32, :], b.BAB)
        b.dma("sync", dbgB[32:64, :], b.BAA)
    bar = T(b.BAB.ap[:, 3:3 + TLOC], b.BAB.res)
    aar = T(b.BAA.ap[:, 3:3 + TLOC], b.BAA.res)
    b.act(b.SCR32, bar, AF.Abs)
    b.act(b.SCR32, b.SCR32, AF.Exp, scale=-1.0)
    b.act(b.SCR32, b.SCR32, AF.Ln, bias=T(b.epsc.ap[0:32, 2:3], b.epsc.res))
    b.ts(b.GLNB, bar, -1.0, 0.0, op0=ALU.mult, op1=ALU.max)
    b.tt(b.GLNB, b.GLNB, b.SCR32, op=ALU.add)
    b.ts(b.GLNB, b.GLNB, -1.0)
    b.dma("sync", T(b.CVEC.ap[0:HV, 0:1], b.CVEC.res), b.dt_bias)
    b.dma("sync", T(b.CVEC.ap[0:HV, 1:2], b.CVEC.res), b.a_log)
    b.dma("sync", T(b.CVEC.ap[0:DV, 2:3], b.CVEC.res), b.norm_w)
    b.dma("sync", T(b.CVEC.ap[:, 3:4], b.CVEC.res), b.half_mask)
    b.dma("sync", T(b.CVEC.ap[:, 4:5], b.CVEC.res), b.first_mask)
    b.act(T(b.CVEC.ap[0:HV, 5:6], b.CVEC.res), T(b.CVEC.ap[0:HV, 1:2], b.CVEC.res), AF.Exp)
    b.ts(T(b.CVEC.ap[0:HV, 5:6], b.CVEC.res), T(b.CVEC.ap[0:HV, 5:6], b.CVEC.res), -1.0)
    b.ts(b.SCR32, aar, T(b.CVEC.ap[0:HV, 0:1], b.CVEC.res), op0=ALU.add)
    b.act(b.G32, b.SCR32, AF.Abs)
    b.act(b.G32, b.G32, AF.Exp, scale=-1.0)
    b.act(b.G32, b.G32, AF.Ln, bias=T(b.epsc.ap[0:32, 2:3], b.epsc.res))
    b.ts(b.SCR32, b.SCR32, 0.0, op0=ALU.max)
    b.tt(b.SCR32, b.SCR32, b.G32, op=ALU.add)
    b.ts(b.SCR32, b.SCR32, T(b.CVEC.ap[0:HV, 5:6], b.CVEC.res), op0=ALU.mult)
    b.vop(lambda e: e.tensor_tensor_scan(out=b.G32.ap, data0=b.ones_scan.ap,
                                         data1=b.SCR32.ap, initial=0.0,
                                         op0=ALU.mult, op1=ALU.add),
          reads=[b.ones_scan, b.SCR32], writes=[b.G32])
    b.tt(b.GLNB, b.GLNB, b.G32, op=ALU.add)
    for ci in range(NCH):
        ce = (ci + 1) * CL
        dst = T(b.DCROW.ap[:, ci:ci + 1], b.DCROW.res)
        if ci == 0:
            b.cp(dst, T(b.G32.ap[:, ce - 1:ce], b.G32.res))
        else:
            b.ts(dst, T(b.G32.ap[:, ce - 1:ce], b.G32.res),
                 T(b.G32.ap[:, ci * CL - 1:ci * CL], b.G32.res), op0=ALU.subtract)
    b.act(b.DCROW, b.DCROW, AF.Exp)
    b.cp(b.DCROWB, b.DCROW)
    for r in range(32):
        pq = b.PQ[24 + (r % 2)]
        b.mm(T(pq.ap[:, 0:NCH], pq.res),
             T(b.selb32.ap[:, r * 128:(r + 1) * 128], b.selb32.res), b.DCROWB)
        b.cp(T(b.DCREP.ap[:, r * NCH:(r + 1) * NCH], b.DCREP.res), T(pq.ap[:, 0:NCH], pq.res))

    dbg = b.dbg_tap("rows", [32, 2 * TLOC])
    if dbg is not None:
        b.dma("sync", dbg[0:32, 0:TLOC], b.G32)
        b.dma("sync", dbg[0:32, TLOC:2 * TLOC], b.GLNB)

    NGRID = [(0, 512, 0), (512, 512, 1), (1024, 3, 2)]
    for h in range(HK):
        for pas in range(2):
            c_lo = 768 * h + 384 * pas
            for qt in range(4):
                k0, nk = qt * 4, 4
                b.dma("gpsimd",
                      rr(T(b.WBUF[qt % 2].ap[:, 0:nk * 384], b.WBUF[qt % 2].res),
                         "p (k o) -> p k o", k=nk),
                      rr(T(b.w_qkvz.ap[128 * k0:128 * (k0 + nk), c_lo:c_lo + 384],
                           b.w_qkvz.res), "(k p) o -> p k o", p=128))
                for ktl in range(nk):
                    kt = k0 + ktl
                    for o in range(3):
                        lhs = b.WBUF[qt % 2][:, ktl * 384 + o * 128: ktl * 384 + (o + 1) * 128]
                        for (c0, w, ni) in NGRID[:2]:
                            ps = T(b.PSB[o * 2 + ni][:, 0:w], b.PFUL[o * 2 + ni].res)
                            b.mm(ps, lhs, XLN[kt][:, c0:c0 + w],
                                 start=(kt == 0), stop=(kt == 15))
                    tail_ps = T(b.PSB[6][:, 0:384], b.PFUL[6].res)
                    b.mm(tail_ps, XLN[kt][:, 899:1027],
                         b.WBUF[qt % 2][:, ktl * 384:(ktl + 1) * 384],
                         start=(kt == 0), stop=(kt == 15))
            tstg = T(b.YLD[0].ap[:, 0:384], b.YLD[0].res)
            b.cp(tstg, T(b.PSB[6][:, 0:384], b.PFUL[6].res))
            for o in range(3):
                gidx = pas * 3 + o
                ptr = T(b.PB8[0].ap[:, 125:128], b.PB8[0].res)
                b.tr(T(b.PB8[0].ap[:, 0:128], b.PB8[0].res),
                     T(b.YLD[0].ap[:, o * 128:(o + 1) * 128], b.YLD[0].res))
                for (c0, w, ni) in NGRID[:2]:
                    src = T(b.PSB[o * 2 + ni][:, 0:w], b.PFUL[o * 2 + ni].res)
                    if gidx < 4:
                        b.cp(MIXT[gidx][:, c0:c0 + w], src)
                    else:
                        b.act(ZROT[gidx - 4][:, c0:c0 + w], src, AF.Silu)
                if gidx < 4:
                    b.cp(MIXT[gidx][:, 1024:1027], ptr)
                else:
                    b.act(ZROT[gidx - 4][:, 1024:1027], ptr, AF.Silu)
                if gidx >= 4:
                    v = 2 * h + (gidx - 4)
                    b.dma("gpsimd", b.z_sp[v], ZROT[gidx - 4][:, 3:3 + TLOC])
            conv_tiles = [0, 1, 2] if pas == 0 else [3]
            for gidx in conv_tiles:
                if gidx == 0:
                    r0 = 128 * h
                elif gidx == 1:
                    r0 = 2048 + 128 * h
                else:
                    r0 = 4096 + 256 * h + 128 * (gidx - 2)
                b.dma("sync", b.WCOL, T(b.conv_w.ap[r0:r0 + 128, :], b.conv_w.res))
                for j in range(4):
                    b.ts(b.DIAG[j], b.ident_bfs, T(b.WCOL.ap[:, j:j + 1], b.WCOL.res),
                         op0=ALU.mult)
                for nn in range(2):
                    ps = T(b.PSB[nn][:, 0:512], b.PFUL[nn].res)
                    for j in range(4):
                        b.mm(ps, b.DIAG[j], MIXT[gidx][:, 512 * nn + j: 512 * nn + j + 512],
                             start=(j == 0), stop=(j == 3))
                    b.act(YT[gidx][:, 3 + 512 * nn: 3 + 512 * (nn + 1)], ps, AF.Silu)
                b.dma("gpsimd", b.y_sp[h * 4 + gidx], YT[gidx][:, 3:3 + TLOC])
                if gidx in (0, 1):
                    b.act(MIXT[gidx][:, 0:TLOC], YT[gidx][:, 3:3 + TLOC], AF.Square)
                    for nn in range(2):
                        ps = T(b.PSB[2 + nn][0:1, 0:512], b.PFUL[2 + nn].res)
                        b.mm(ps, b.ones_col_bfs, MIXT[gidx][:, nn * 512:(nn + 1) * 512])
                        b.cp(b.ROW[0:1, nn * 512:(nn + 1) * 512], ps)
                    dst = b.SQQ2 if gidx == 0 else b.SQK2
                    b.dma("sync", T(dst.ap[h:h + 1, 0:TLOC], dst.res),
                          b.ROW[0:1, 0:TLOC])

    dbgY = b.dbg_tap("ysp", [6 * 128, TLOC], BF16)
    if dbgY is not None:
        for g in range(4):
            b.dma("sync", dbgY[g * 128:(g + 1) * 128, :], b.y_sp[g])
        for v in range(2):
            b.dma("sync", dbgY[(4 + v) * 128:(5 + v) * 128, :], b.z_sp[v])

    b.act(b.SQQ2, b.SQQ2, AF.Ln, bias=T(b.epsc.ap[0:16, 1:2], b.epsc.res))
    b.ts(b.SQQ2, b.SQQ2, -0.5, -0.5 * float(np.log(128.0)), op0=ALU.mult, op1=ALU.add)
    b.act(b.SQK2, b.SQK2, AF.Ln, bias=T(b.epsc.ap[0:16, 1:2], b.epsc.res))
    b.ts(b.SQK2, b.SQK2, -0.5)

    b.fence()
    SLOC = [b.pb(vr_ * 128, 128, tag=("sloc", vr_)) for vr_ in range(HV)]
    for v_ in range(HV):
        b.vop(lambda e, v_=v_: e.memset(SLOC[v_].ap, 0.0), writes=[SLOC[v_]])
    for ci in range(NCH):
        cs = ci * CL
        if ci == 0:
            b.cp(b.REBA, T(b.G32.ap[:, cs:cs + CL], b.G32.res))
            b.cp(b.REBB, T(b.GLNB.ap[:, cs:cs + CL], b.GLNB.res))
        else:
            gcs = T(b.G32.ap[:, cs - 1:cs], b.G32.res)
            b.ts(b.REBA, T(b.G32.ap[:, cs:cs + CL], b.G32.res), gcs, op0=ALU.subtract)
            b.ts(b.REBB, T(b.GLNB.ap[:, cs:cs + CL], b.GLNB.res), gcs, op0=ALU.subtract)
        gce = T(b.G32.ap[:, cs + CL - 1:cs + CL], b.G32.res)
        b.ts(b.REBC, T(b.G32.ap[:, cs:cs + CL], b.G32.res),
             gce, -1.0, op0=ALU.subtract, op1=ALU.mult)
        b.dma("sync", T(b.STK.ap[0:32, :], b.STK.res), b.REBA)
        b.dma("sync", T(b.STK.ap[32:64, :], b.STK.res), b.REBB)
        b.dma("sync", T(b.STK.ap[64:96, :], b.STK.res), b.REBC)
        b.dma("sync", T(b.STK.ap[96:112, :], b.STK.res),
              T(b.SQQ2.ap[:, cs:cs + CL], b.SQQ2.res))
        b.dma("sync", T(b.STK.ap[112:128, :], b.STK.res),
              T(b.SQK2.ap[:, cs:cs + CL], b.SQK2.res))
        pq = b.PQ[24]
        b.tr(pq, b.STK, f32=True)
        b.cp(b.COLS, pq)
        b.tt(T(b.EARG0.ap[:, 0:16], b.EARG0.res), T(b.COLS.ap[:, 0:16], b.COLS.res),
             T(b.COLS.ap[:, 96:112], b.COLS.res), op=ALU.add)
        b.tt(T(b.EARG0.ap[:, 16:32], b.EARG0.res), T(b.COLS.ap[:, 16:32], b.COLS.res),
             T(b.COLS.ap[:, 96:112], b.COLS.res), op=ALU.add)
        b.cp(T(b.EARG0.ap[:, 32:96], b.EARG0.res), T(b.COLS.ap[:, 32:96], b.COLS.res))
        b.tt(T(b.EARG0.ap[:, 96:128], b.EARG0.res), T(b.COLS.ap[:, 32:64], b.COLS.res),
             T(b.COLS.ap[:, 0:32], b.COLS.res), op=ALU.subtract)
        b.act(b.ECOL, b.EARG0, AF.Exp)
        for (src, dsts) in ((b.REBA, b.CTRIG), (b.REBB, b.CTRIB)):
            b.cp(dsts[0], src)
            b.tt(b.TRIF[0], src, dsts[0], op=ALU.subtract)
            b.cp(dsts[1], b.TRIF[0])
            b.tt(b.TRIF[1], b.TRIF[0], dsts[1], op=ALU.subtract)
            b.cp(dsts[2], b.TRIF[1])
        tf0 = T(b.TRIF2[0].ap[0:16, :], b.TRIF2[0].res)
        tf1 = T(b.TRIF2[1].ap[0:16, :], b.TRIF2[1].res)
        for (srcx, dsts) in ((T(b.SQQ2.ap[:, cs:cs + CL], b.SQQ2.res), b.CTRI2Q),
                             (T(b.SQK2.ap[:, cs:cs + CL], b.SQK2.res), b.CTRI2K)):
            b.cp(dsts[0], srcx)
            b.tt(tf0, srcx, dsts[0], op=ALU.subtract)
            b.cp(dsts[1], tf0)
            b.tt(tf1, tf0, dsts[1], op=ALU.subtract)
            b.cp(dsts[2], tf1)

        for h in range(HK):
            par = h % 2
            YL = b.YLD[0]
            for g in range(4):
                b.dma("sync", YL[:, g * 128:(g + 1) * 128],
                      T(b.y_sp[h * 4 + g].ap[:, cs:cs + CL], b.y_sp[h * 4 + g].res))
            pq0 = b.PQ[0]
            for lvl in range(3):
                b.mm(pq0, sel16(b, h), b.CTRI2K[lvl], start=(lvl == 0), stop=(lvl == 2))
            b.act(b.EB[0], pq0, AF.Exp)
            pq1 = b.PQ[1]
            for lvl in range(3):
                b.mm(pq1, sel16(b, h), b.CTRI2Q[lvl], start=(lvl == 0), stop=(lvl == 2))
            b.act(b.EB[1], pq1, AF.Exp)
            b.tt(b.KNF[0], YL[:, 128:256], b.EB[0])
            b.tt(b.QNF[0], YL[:, 0:128], b.EB[1])
            pb0 = b.PB8[0]
            b.tr(pb0, b.KNF[0])
            b.cp(b.KTM[0], pb0)
            pb1 = b.PB8[1]
            b.tr(pb1, YL[:, 256:384])
            b.cp(T(b.VTM[0].ap[:, 0:128], b.VTM[0].res), pb1)
            pb2 = b.PB8[2]
            b.tr(pb2, YL[:, 384:512])
            b.cp(T(b.VTM[0].ap[:, 128:256], b.VTM[0].res), pb2)
            pq2 = b.PQ[2]
            b.mm(pq2, b.KNF[0], b.KNF[0])
            b.ts(b.KKT[0], pq2, -1.0)
            pq3 = b.PQ[3]
            b.mm(pq3, b.KNF[0], b.QNF[0])
            b.cp(b.KQT[0], pq3)
            for j in range(2):
                vhr = h + 16 * j
                ecol = lambda c0_, vhr=vhr: T(b.ECOL.ap[:, c0_ + vhr:c0_ + vhr + 1], b.ECOL.res)
                gcol = T(b.COLS.ap[:, vhr:vhr + 1], b.COLS.res)
                b.ts(T(b.BVK[j].ap[:, 0:128], b.BVK[j].res),
                     T(b.VTM[0].ap[:, j * 128:(j + 1) * 128], b.VTM[0].res), ecol(96))
                b.ts(T(b.BVK[j].ap[:, 128:256], b.BVK[j].res), b.KTM[0], ecol(32))
                b.ts(b.BKD[j], b.KTM[0], ecol(64))
                qb = (2 + j) * 4
                pqa = b.PQ[qb + 0]
                for lvl in range(3):
                    b.mm(pqa, T(b.selb32.ap[:, vhr * 128:(vhr + 1) * 128], b.selb32.res),
                         b.CTRIG[lvl],
                         start=(lvl == 0), stop=False)
                for lvl in range(3):
                    b.mm(pqa, sel16(b, h), b.CTRI2Q[lvl],
                         start=False, stop=(lvl == 2))
                b.act(b.ARG[j], pqa, AF.Exp)
                b.tt(b.QHF[j], YL[:, 0:128], b.ARG[j])
                pqb = b.PQ[qb + 1]
                for lvl in range(3):
                    b.mm(pqb, T(b.selb32.ap[:, vhr * 128:(vhr + 1) * 128], b.selb32.res),
                         b.CTRIB[lvl],
                         start=(lvl == 0), stop=False)
                b.mm(pqb, b.ident_bfs, b.maskb_strict, start=False, stop=True)
                b.ts(b.ARG[j], pqb, gcol, op0=ALU.subtract)
                b.act(b.EARGB[j], b.ARG[j], AF.Exp)
                b.tt(b.MTP[2 * j], b.EARGB[j], b.KKT[0])
                pqc = b.PQ[qb + 2]
                for lvl in range(3):
                    b.mm(pqc, T(b.selb32.ap[:, vhr * 128:(vhr + 1) * 128], b.selb32.res),
                         b.CTRIG[lvl],
                         start=(lvl == 0), stop=False)
                b.mm(pqc, b.ident_bfs, b.maskb_incl, start=False, stop=True)
                b.ts(b.ARG[j], pqc, gcol, op0=ALU.subtract)
                b.act(b.EARGB[j], b.ARG[j], AF.Exp)
                b.tt(b.WOTT[j], b.EARGB[j], b.KQT[0])
                rsrc = b.BVK[j]
                mt = b.MTP[2 * j]
                for lvl in range(SOLVE_LEVELS):
                    ph = b.PH[13]
                    b.mm(ph, mt, rsrc, start=True, stop=False)
                    b.mm(ph, b.ident_bfs, rsrc, start=False, stop=True)
                    rdst = b.RSOL[2 * j + (lvl % 2)]
                    b.cp(rdst, ph)
                    if lvl < SOLVE_LEVELS - 1:
                        pbx = b.PB8[3]
                        b.tr(pbx, mt)
                        b.cp(b.MXP[j], pbx)
                        pqd = b.PQ[qb + 3]
                        b.mm(pqd, b.MXP[j], mt)
                        mtn = b.MTP[2 * j + 1] if mt is b.MTP[2 * j] else b.MTP[2 * j]
                        b.cp(mtn, pqd)
                        mt = mtn
                    rsrc = rdst
                R = rsrc
                RP = T(R.ap[:, 0:128], R.res)
                RZ = T(R.ap[:, 128:256], R.res)
                pqe = b.PQ[qb + 0]
                b.mm(pqe, RZ, b.BKD[j])
                b.ts(T(b.EFUW[j].ap[:, 0:128], b.EFUW[j].res), pqe, -1.0)
                pqf = b.PQ[qb + 1]
                b.mm(pqf, b.BKD[j], RP)
                b.cp(T(b.EFUW[j].ap[:, 128:256], b.EFUW[j].res), pqf)
                pqg = b.PQ[qb + 2]
                b.mm(pqg, RZ, b.WOTT[j])
                b.tt(T(b.EFUW[j].ap[:, 256:384], b.EFUW[j].res), b.QHF[j], pqg,
                     op=ALU.subtract)
                pqh = b.PQ[qb + 3]
                b.mm(pqh, RP, b.WOTT[j])
                b.cp(T(b.EFUW[j].ap[:, 384:512], b.EFUW[j].res), pqh)
                psc = b.PQ[25]
                b.mm(psc, T(b.EFUW[j].ap[:, 0:128], b.EFUW[j].res), SLOC[vhr],
                     start=True, stop=False)
                b.mm(psc, b.ident_bfs, T(b.EFUW[j].ap[:, 128:256], b.EFUW[j].res),
                     start=False, stop=True)
                b.stt(SLOC[vhr], SLOC[vhr],
                      T(b.DCREP.ap[:, vhr * NCH + ci:vhr * NCH + ci + 1], b.DCREP.res),
                      psc, op0=ALU.mult, op1=ALU.add)
                b.dma("gpsimd", b.spill[vhr * NCH + ci], b.EFUW[j])

    b.fence()
    for v_ in range(HV):
        b.dma("sync", T(b.agA_in.ap[vr(v_) * 128:(vr(v_) + 1) * 128, :], b.agA_in.res),
              SLOC[vr(v_)])
    b.gop(lambda e: e.collective_compute(
        "AllGather", ALU.bypass, replica_groups=b.RG,
        ins=[b.agA_in.ap[:, :]], outs=[b.agA_out.ap[:, :]]),
        reads=[b.agA_in], writes=[b.agA_out])
    for v_ in range(HV):
        r_ = vr(v_)
        stg = b.YLD[0]
        b.dma("sync", stg[:, 0:128],
              T(b.agA_out.ap[r_ * 128:(r_ + 1) * 128, :], b.agA_out.res))
        b.ts(SLOC[r_], stg[:, 0:128], T(b.CVEC.ap[:, 3:4], b.CVEC.res), op0=ALU.mult)

    b.fence()
    ZP = [b.pb(MIX_OFF + 2 * 1056 + j * 128, 128, tag=("zp", j)) for j in range(2)]
    GTO = [b.pb(MIX_OFF + 2 * 1056 + 256 + j * 128, 128, tag=("gto", j)) for j in range(2)]
    LEFU = b.EFUW
    GTS = b.EARGB
    SQG = b.MXP
    for v_ in range(HV):
        r_ = vr(v_)
        for ci in range(NCH):
            par = ci % 2
            b.dma("gpsimd", LEFU[par], b.spill[r_ * NCH + ci])
            b.dma("sync", ZP[par], T(b.z_sp[v_].ap[:, ci * CL:(ci + 1) * CL], b.z_sp[v_].res))
            pq = b.PQ[par * 4 + 0]
            b.mm(pq, SLOC[r_], T(LEFU[par].ap[:, 256:384], LEFU[par].res),
                 start=True, stop=False)
            b.mm(pq, b.ident_bfs, T(LEFU[par].ap[:, 384:512], LEFU[par].res),
                 start=False, stop=True)
            b.tt(GTS[par], pq, ZP[par])
            b.act(SQG[par], GTS[par], AF.Square)
            pr = b.PQ[par * 4 + 1]
            b.mm(T(pr.ap[0:1, :], pr.res), b.ones_col_bfs, SQG[par])
            b.act(T(b.ROW2.ap[0:1, 0:128], b.ROW2.res), T(pr.ap[0:1, :], pr.res),
                  AF.Ln, bias=T(b.epsc.ap[0:1, 1:2], b.epsc.res),
                  scale=T(b.epsc.ap[0:1, 3:4], b.epsc.res))
            b.act(T(b.ROW2.ap[0:1, 128:256], b.ROW2.res),
                  T(b.ROW2.ap[0:1, 0:128], b.ROW2.res), AF.Exp, scale=-0.5)
            pbc = b.PQ[par * 4 + 2]
            b.mm(pbc, b.ones_row_f32s, T(b.ROW2.ap[0:1, 128:256], b.ROW2.res))
            b.stt(GTO[par], pbc, T(b.CVEC.ap[:, 2:3], b.CVEC.res), GTS[par],
                  op0=ALU.mult, op1=ALU.mult)
            b.dma("sync", T(b.gated_sp[v_].ap[:, ci * CL:(ci + 1) * CL], b.gated_sp[v_].res),
                  GTO[par])
            psc = b.PQ[25]
            b.mm(psc, T(LEFU[par].ap[:, 0:128], LEFU[par].res), SLOC[r_],
                 start=True, stop=False)
            b.mm(psc, b.ident_bfs, T(LEFU[par].ap[:, 128:256], LEFU[par].res),
                 start=False, stop=True)
            b.stt(SLOC[r_], SLOC[r_],
                  T(b.DCREP.ap[:, r_ * NCH + ci:r_ * NCH + ci + 1], b.DCREP.res),
                  psc, op0=ALU.mult, op1=ALU.add)

    dbgG = b.dbg_tap("gated", [HV * 128, TLOC], BF16)
    if dbgG is not None:
        for v_ in range(HV):
            b.dma("sync", dbgG[v_ * 128:(v_ + 1) * 128, :], b.gated_sp[v_])

    b.fence()
    GBUF = [b.pb(XLN_OFF + 4096 + j * 512, 512, tag=("gbuf", j)) for j in range(2)]
    OGS = [(0, 7), (7, 7), (14, 2)]
    for (o0, no) in OGS:
        for th in range(2):
            t0 = th * 512
            for kt in range(32):
                b.dma("sync", GBUF[kt % 2],
                      T(b.gated_sp[kt].ap[:, t0:t0 + 512], b.gated_sp[kt].res))
                b.dma("gpsimd",
                      rr(T(b.WBUF[kt % 2].ap[:, 0:no * 128], b.WBUF[kt % 2].res),
                         "p (o c) -> p o c", o=no),
                      rr(T(b.w_out.ap[kt * 128:(kt + 1) * 128,
                           o0 * 128:(o0 + no) * 128], b.w_out.res),
                         "p (o c) -> p o c", o=no))
                for o in range(no):
                    b.mm(T(b.PSB[o][:, 0:512], b.PFUL[o].res),
                         b.WBUF[kt % 2][:, o * 128:(o + 1) * 128], GBUF[kt % 2],
                         start=(kt == 0), stop=(kt == 31))
            for o in range(no):
                oc = o0 + o
                b.tt(b.X[oc][:, 3 + t0:3 + t0 + 512], b.X[oc][:, 3 + t0:3 + t0 + 512],
                     T(b.PSB[o][:, 0:512], b.PFUL[o].res), op=ALU.add)

    dbgX = b.dbg_tap("x_attnA", [DM, TLOC], F32)
    if dbgX is not None:
        for i in range(16):
            b.dma("sync", dbgX[i * 128:(i + 1) * 128, :], b.X[i][:, 3:3 + TLOC])

def emit_ffn(b, first):
    b.fence()
    emit_ln(b, c0=3, ncols=TLOC)
    XDB = [b.pb(i * 1024, 1024, tag=("xdb", i)) for i in range(16)]
    OCAST = [b.pb(16896 + j * 1024, 1024, tag=("oc8", j)) for j in range(2)]
    WDB = b.pb(19000, 2048, tag="wdb")
    XTF = b.ARG[0]
    SCC = b.ECOL
    if first:
        for kt in range(16):
            for hh in range(2):
                wf = T(b.PF32[hh].ap[:, 0:1024], b.PF32[hh].res)
                b.dma("sync", wf, T(b.w_ffn.ap[kt * 128:(kt + 1) * 128,
                                               hh * 1024:(hh + 1) * 1024], b.w_ffn.res))
                sq = T(b.PF32[2].ap[:, 0:1024], b.PF32[2].res)
                b.act(sq, wf, AF.Square)
                b.vop(lambda e, sq=sq: e.tensor_reduce(
                    out=b.CVECB.ap[:, 0:8],
                    in_=sq.ap.rearrange("p (c n) -> p c n", c=8),
                    axis=AX.X, op=ALU.max), reads=[b.PF32[2]], writes=[b.CVECB])
                pq = b.PQ[24]
                b.tr(T(pq.ap[0:8, 0:128], pq.res), T(b.CVECB.ap[:, 0:8], b.CVECB.res), f32=True)
                b.cp(T(b.ECOL.ap[0:8, 0:128], b.ECOL.res), T(pq.ap[0:8, 0:128], pq.res))
                b.vop(lambda e: e.tensor_reduce(
                    out=b.ECOL.ap[0:8, 126:127], in_=b.ECOL.ap[0:8, 0:128],
                    axis=AX.X, op=ALU.max), reads=[b.ECOL], writes=[b.ECOL])
                mc = T(b.ECOL.ap[0:8, 126:127], b.ECOL.res)
                b.ts(mc, mc, 1e-8, op0=ALU.max)
                b.act(mc, mc, AF.Ln)
                b.act(mc, mc, AF.Exp, scale=-0.5)
                b.tr(T(pq.ap[0:1, 8:16], pq.res), mc, f32=True, p=8)
                b.cp(T(b.ECOL.ap[0:1, 16:24], b.ECOL.res), T(pq.ap[0:1, 8:16], pq.res))
                pb2 = b.PQ[25]
                b.mm(T(pb2.ap[:, 0:8], pb2.res), b.ones_row_f32s,
                     T(b.ECOL.ap[0:1, 16:24], b.ECOL.res))
                b.cp(T(b.ECOL.ap[:, 24:32], b.ECOL.res), T(pb2.ap[:, 0:8], pb2.res))
                inv = T(b.ECOL.ap[:, 32:40], b.ECOL.res)
                b.vop(lambda e: e.reciprocal(out=b.ECOL.ap[:, 32:40],
                                             in_=b.ECOL.ap[:, 24:32]),
                      reads=[b.ECOL], writes=[b.ECOL])
                b.ts(T(b.ECOL.ap[:, 24:32], b.ECOL.res),
                     T(b.ECOL.ap[:, 24:32], b.ECOL.res), 224.0)
                b.ts(inv, inv, 1.0 / 224.0)
                for blk in range(8):
                    b.ts(T(b.XQ8[0].ap[:, blk * 128:(blk + 1) * 128], b.XQ8[0].res),
                         T(wf.ap[:, blk * 128:(blk + 1) * 128], wf.res),
                         T(b.ECOL.ap[:, 24 + blk:25 + blk], b.ECOL.res))
                    b.ts(T(WDB.ap[:, hh * 1024 + blk * 128:hh * 1024 + (blk + 1) * 128],
                           WDB.res),
                         T(b.XQ8[0].ap[:, blk * 128:(blk + 1) * 128], b.XQ8[0].res),
                         T(b.ECOL.ap[:, 32 + blk:33 + blk], b.ECOL.res))
            b.dma("gpsimd", T(b.wd_sp.ap[kt * 128:(kt + 1) * 128, :], b.wd_sp.res), WDB)
    for i in range(16):
        xf = T(b.PF32[0].ap[:, 0:TLOC], b.PF32[0].res)
        b.tt(xf, b.X[i][:, 3:3 + TLOC], b.RSTDB[:, 0:TLOC])
        b.tt(xf, xf, b.NMRB[:, 0:TLOC], op=ALU.add)
        for blk in range(8):
            pq = b.PQ[24]
            b.tr(pq, T(xf.ap[:, blk * 128:(blk + 1) * 128], b.PF32[0].res), f32=True)
            b.cp(XTF, pq)
            b.act(b.EARGB[0], XTF, AF.Square)
            b.vop(lambda e: e.tensor_reduce(
                out=SCC.ap[:, 0:1], in_=b.EARGB[0].ap, axis=AX.X, op=ALU.max),
                reads=[b.EARGB[0]], writes=[SCC])
            sc = T(SCC.ap[:, 0:1], SCC.res)
            b.ts(sc, sc, 1e-8, op0=ALU.max)
            b.act(sc, sc, AF.Ln)
            b.act(sc, sc, AF.Exp, scale=-0.5)
            b.ts(sc, sc, 224.0)
            b.ts(T(b.XQ8[0].ap[:, 0:128], b.XQ8[0].res), XTF, sc)
            b.vop(lambda e: e.reciprocal(out=SCC.ap[:, 1:2], in_=SCC.ap[:, 0:1]),
                  reads=[SCC], writes=[SCC])
            b.ts(b.EARGB[0], T(b.XQ8[0].ap[:, 0:128], b.XQ8[0].res),
                 T(SCC.ap[:, 1:2], SCC.res))
            pb2 = T(b.PB8[0].ap, b.PB8[0].res)
            b.tr(pb2, b.EARGB[0])
            b.cp(XDB[i][:, blk * 128:(blk + 1) * 128], pb2)
    for oc in range(16):
        b.dma("sync", rr(T(b.WBUF[oc % 2].ap[:, 0:2048], b.WBUF[oc % 2].res),
                         "p (k o) -> p k o", k=16),
              rr(T(b.wd_sp.ap[:, oc * 128:(oc + 1) * 128], b.wd_sp.res),
                 "(k p) o -> p k o", p=128))
        for nn in range(2):
            ps = T(b.PSB[4 + oc % 2][:, 0:512], b.PFUL[4 + oc % 2].res)
            for kt in range(16):
                b.mm(ps, b.WBUF[oc % 2][:, kt * 128:(kt + 1) * 128],
                     XDB[kt][:, nn * 512:(nn + 1) * 512],
                     start=(kt == 0), stop=(kt == 15))
            b.cp(OCAST[oc % 2][:, nn * 512:(nn + 1) * 512], ps)
        b.tt(b.X[oc][:, 3:3 + TLOC], b.X[oc][:, 3:3 + TLOC], OCAST[oc % 2], op=ALU.add)
    dbgX = b.dbg_tap("x_ffn1" if first else "x_ffn2", [DM, TLOC], F32)
    if dbgX is not None:
        for i in range(16):
            b.dma("sync", dbgX[i * 128:(i + 1) * 128, :], b.X[i][:, 3:3 + TLOC])


def emit_attn_b(b):
    b.fence()
    CB = b.CVECB
    for (cix, src) in ((0, b.ln1_w), (1, b.ln1_b), (2, b.ln2_w), (3, b.ln2_b),
                       (4, b.time_decay), (5, b.time_first)):
        b.dma("sync", T(CB.ap[:, 32 + cix * 16:32 + (cix + 1) * 16], CB.res), src)
    b.act(T(CB.ap[:, 96:112], CB.res), T(CB.ap[:, 96:112], CB.res), AF.Exp)
    b.ts(T(CB.ap[:, 96:112], CB.res), T(CB.ap[:, 96:112], CB.res), -1.0)
    b.act(T(CB.ap[:, 0:16], CB.res), T(CB.ap[:, 96:112], CB.res), AF.Exp)
    b.act(T(CB.ap[:, 16:32], CB.res), T(CB.ap[:, 112:128], CB.res), AF.Exp)
    b.ts(T(CB.ap[:, 16:32], CB.res), T(CB.ap[:, 16:32], CB.res), float(2.0 ** -32))

    emit_ln(b, c0=3, ncols=TLOC)
    XL = emit_xln(b, c0=3, ncols=TLOC,
                  aff=(T(CB.ap[:, 32:48], CB.res), T(CB.ap[:, 48:64], CB.res)))
    for i in range(16):
        b.dma("gpsimd", b.xres_sp[i], b.X[i][:, 3:3 + TLOC])
    for i in range(16):
        xf = T(b.PF32[i % 2].ap[:, 0:TLOC], b.PF32[i % 2].res)
        b.tt(xf, b.X[i][:, 3:3 + TLOC], b.RSTDB[:, 0:TLOC])
        b.tt(b.X[i][:, 3:3 + TLOC], xf, b.NMRB[:, 0:TLOC], op=ALU.add)
    EKR = [b.pb(16896 + j * 1024, 1024, tag=("abr", j)) for j in range(2)]
    UR = [b.pb(16896 + (2 + j) * 1024, 1024, tag=("abr", 2 + j)) for j in range(2)]
    WKR = [b.pb(16896 + (4 + j) * 1024, 1024, tag=("abr", 4 + j)) for j in range(2)]
    SGR = [b.pb(16896 + (6 + j) * 1024, 1024, tag=("abr", 6 + j)) for j in range(2)]
    RMP = T(b.PF32[2].ap[:, 0:TLOC], b.PF32[2].res)
    SC1 = T(b.PF32[1].ap[:, 0:TLOC], b.PF32[1].res)
    b.vop(lambda e: e.memset(SC1.ap, 1.0), writes=[b.PF32[1]])
    b.vop(lambda e: e.tensor_tensor_scan(out=RMP.ap, data0=SC1.ap,
                                         data1=SC1.ap, initial=0.0,
                                         op0=ALU.mult, op1=ALU.add),
          reads=[b.PF32[1]], writes=[b.PF32[2]])

    def proj(w_dram, oc, evac, pbank):
        b.dma("gpsimd", rr(T(b.WBUF[oc % 2].ap[:, 0:2048], b.WBUF[oc % 2].res),
                           "p (k o) -> p k o", k=16),
              rr(T(w_dram.ap[:, oc * 128:(oc + 1) * 128], w_dram.res),
                 "(k p) o -> p k o", p=128))
        for nn in range(2):
            ps = T(b.PSB[pbank + oc % 2][:, 0:512], b.PFUL[pbank + oc % 2].res)
            for kt in range(16):
                b.mm(ps, b.WBUF[oc % 2][:, kt * 128:(kt + 1) * 128],
                     XL[kt][:, nn * 512:(nn + 1) * 512],
                     start=(kt == 0), stop=(kt == 15))
            evac(nn, ps)

    for oc in range(16):
        par = oc % 2
        proj(b.w_k, oc,
             lambda nn, ps, par=par: b.act(EKR[par][:, nn * 512:(nn + 1) * 512], ps, AF.Exp,
                                           bias=T(b.epsc.ap[:, 4:5], b.epsc.res)), 0)

        def vev(nn, ps, oc=oc, par=par):
            b.tt(UR[par][:, nn * 512:(nn + 1) * 512],
                 EKR[par][:, nn * 512:(nn + 1) * 512], ps)
            if nn == 0:
                b.cp(T(b.VC0.ap[:, oc:oc + 1], b.VC0.res), T(ps.ap[:, 0:1], ps.res))
        proj(b.w_v, oc, vev, 2)
        dcol = T(CB.ap[:, oc:oc + 1], CB.res)
        b.ts(SC1, RMP, 0.0, dcol, op0=ALU.mult, op1=ALU.add)
        b.vop(lambda e, oc=oc, par=par: e.tensor_tensor_scan(
            out=WKR[par].ap, data0=SC1.ap, data1=UR[par].ap,
            initial=0.0, op0=ALU.mult, op1=ALU.add),
            reads=[b.PF32[1], UR[par]], writes=[WKR[par]])
        b.cp(T(b.AGST.ap[:, oc:oc + 1], b.AGST.res),
             T(WKR[par].ap[:, 1023:1024], WKR[par].res))
        b.dma("gpsimd", b.wkv_sp[oc], WKR[par])
    b.dma("sync", b.agB_in, b.AGST)
    b.gop(lambda e: e.collective_compute(
        "AllGather", ALU.bypass, replica_groups=b.RG,
        ins=[b.agB_in.ap[:, :]], outs=[b.agB_out.ap[:, :]]),
        reads=[b.agB_in], writes=[b.agB_out])
    b.dma("sync", b.AGST, T(b.agB_out.ap[0:128, :], b.agB_out.res))
    b.ts(b.AGST, b.AGST, T(b.CVEC.ap[:, 3:4], b.CVEC.res), op0=ALU.mult)
    for oc in range(16):
        par = oc % 2
        b.dma("sync", WKR[par], b.wkv_sp[oc])
        b.ts(SC1, RMP, T(CB.ap[:, 96 + oc:97 + oc], CB.res), op0=ALU.mult)
        b.act(SC1, SC1, AF.Exp)
        b.stt(WKR[par], SC1, T(b.AGST.ap[:, oc:oc + 1], b.AGST.res), WKR[par],
              op0=ALU.mult, op1=ALU.add)
        t1 = T(b.PF32[1].ap[:, 0:1], b.PF32[1].res)
        b.tt(t1, T(CB.ap[:, 16 + oc:17 + oc], CB.res), T(b.VC0.ap[:, oc:oc + 1], b.VC0.res))
        b.tt(t1, t1, T(b.CVEC.ap[:, 4:5], b.CVEC.res))
        b.tt(T(WKR[par].ap[:, 0:1], WKR[par].res), T(WKR[par].ap[:, 0:1], WKR[par].res),
             t1, op=ALU.add)

        def rev(nn, ps, oc=oc, par=par):
            b.act(SGR[par][:, nn * 512:(nn + 1) * 512], ps, AF.Sigmoid)
            b.stt(EKR[par][:, nn * 512:(nn + 1) * 512],
                  SGR[par][:, nn * 512:(nn + 1) * 512], float(2.0 ** 32),
                  WKR[par][:, nn * 512:(nn + 1) * 512], op0=ALU.mult, op1=ALU.mult)
        proj(b.w_r, oc, rev, 4)
        b.dma("gpsimd", b.rwkv_sp[oc], EKR[par])
    b.fence()
    GBUF = [b.pb(16896 + j * 512, 512, tag=("gb2", j)) for j in range(2)]
    for (o0, no) in [(0, 7), (7, 7), (14, 2)]:
        for th in range(2):
            t0 = th * 512
            for kt in range(16):
                b.dma("sync", GBUF[kt % 2],
                      T(b.rwkv_sp[kt].ap[:, t0:t0 + 512], b.rwkv_sp[kt].res))
                b.dma("gpsimd",
                      rr(T(b.WBUF[kt % 2].ap[:, 0:no * 128], b.WBUF[kt % 2].res),
                         "p (o c) -> p o c", o=no),
                      rr(T(b.w_o.ap[kt * 128:(kt + 1) * 128,
                           o0 * 128:(o0 + no) * 128], b.w_o.res),
                         "p (o c) -> p o c", o=no))
                for o in range(no):
                    b.mm(T(b.PSB[o][:, 0:512], b.PFUL[o].res),
                         b.WBUF[kt % 2][:, o * 128:(o + 1) * 128], GBUF[kt % 2],
                         start=(kt == 0), stop=(kt == 15))
            for o in range(no):
                oc = o0 + o
                b.tt(b.X[oc][:, 3 + t0:3 + t0 + 512], b.X[oc][:, 3 + t0:3 + t0 + 512],
                     T(b.PSB[o][:, 0:512], b.PFUL[o].res), op=ALU.add)

    b.fence()
    emit_ln(b, c0=3, ncols=TLOC)
    XL2 = emit_xln(b, c0=3, ncols=TLOC,
                   aff=(T(CB.ap[:, 64:80], CB.res), T(CB.ap[:, 80:96], CB.res)))
    for oc in range(DFF // 128):
        b.dma("gpsimd", rr(T(b.WBUF[oc % 2].ap[:, 0:2048], b.WBUF[oc % 2].res),
                           "p (k o) -> p k o", k=16),
              rr(T(b.w_fk.ap[:, oc * 128:(oc + 1) * 128], b.w_fk.res),
                 "(k p) o -> p k o", p=128))
        for nn in range(2):
            ps = T(b.PSB[oc % 4][:, 0:512], b.PFUL[oc % 4].res)
            for kt in range(16):
                b.mm(ps, b.WBUF[oc % 2][:, kt * 128:(kt + 1) * 128],
                     XL2[kt][:, nn * 512:(nn + 1) * 512],
                     start=(kt == 0), stop=(kt == 15))
            b.act(b.HR2[oc % 2], ps, AF.Relu)
            b.tt(T(b.pb(16896 + (oc % 2) * 512, 512, tag=("h2r", oc % 2)).ap, 
                   b.pb(16896 + (oc % 2) * 512, 512, tag=("h2r", oc % 2)).res),
                 b.HR2[oc % 2], b.HR2[oc % 2])
            b.dma("gpsimd", T(b.h2_sp[oc].ap[:, nn * 512:(nn + 1) * 512], b.h2_sp[oc].res),
                  b.pb(16896 + (oc % 2) * 512, 512, tag=("h2r", oc % 2)))
    b.fence()
    H2B = [b.pb(16896 + j * 512, 512, tag=("h2b", j)) for j in range(2)]
    for (o0, no) in [(0, 7), (7, 7), (14, 2)]:
        for th in range(2):
            t0 = th * 512
            for kt in range(DFF // 128):
                b.dma("sync", H2B[kt % 2],
                      T(b.h2_sp[kt].ap[:, t0:t0 + 512], b.h2_sp[kt].res))
                b.dma("gpsimd",
                      rr(T(b.WBUF[kt % 2].ap[:, 0:no * 128], b.WBUF[kt % 2].res),
                         "p (o c) -> p o c", o=no),
                      rr(T(b.w_fv.ap[kt * 128:(kt + 1) * 128,
                           o0 * 128:(o0 + no) * 128], b.w_fv.res),
                         "p (o c) -> p o c", o=no))
                for o in range(no):
                    b.mm(T(b.PSB[o][:, 0:512], b.PFUL[o].res),
                         b.WBUF[kt % 2][:, o * 128:(o + 1) * 128], H2B[kt % 2],
                         start=(kt == 0), stop=(kt == 63))
            for o in range(no):
                oc = o0 + o
                b.tt(b.X[oc][:, 3 + t0:3 + t0 + 512], b.X[oc][:, 3 + t0:3 + t0 + 512],
                     T(b.PSB[o][:, 0:512], b.PFUL[o].res), op=ALU.add)
    for i in range(16):
        xf = T(b.PF32[i % 2].ap[:, 0:TLOC], b.PF32[i % 2].res)
        b.dma("sync", xf, b.xres_sp[i])
        b.tt(b.X[i][:, 3:3 + TLOC], b.X[i][:, 3:3 + TLOC], xf, op=ALU.add)
    dbgX = b.dbg_tap("x_attnB", [DM, TLOC], F32)
    if dbgX is not None:
        for i in range(16):
            b.dma("sync", dbgX[i * 128:(i + 1) * 128, :], b.X[i][:, 3:3 + TLOC])
    dbgW = b.dbg_tap("wkv", [DM, TLOC], BF16)
    if dbgW is not None:
        for i in range(16):
            b.dma("sync", dbgW[i * 128:(i + 1) * 128, :], b.wkv_sp[i])


def emit_output(b):
    b.fence()
    marks = []
    for i in range(16):
        marks.append(b.dma("sync", T(b.x_out.ap[i * 128:(i + 1) * 128, :], b.x_out.res),
                           b.X[i][:, 3:3 + TLOC]))
    for eng in ENGINES:
        for (sem, val) in marks:
            b._wait(eng, sem, val)


def build(dbg=(), phases=("attn_a", "ffn1", "attn_b", "ffn2")):
    b = B(dbg)
    b.load_consts()
    for i in range(16):
        b.dma("sync", b.X[i], T(b.x_in.ap[128 * i:128 * (i + 1), :], b.x_in.res))
    if "attn_a" in phases:
        emit_attn_a(b)
    if "ffn1" in phases:
        emit_ffn(b, first=True)
    if "attn_b" in phases:
        emit_attn_b(b)
    if "ffn2" in phases:
        emit_ffn(b, first="ffn1" not in phases)
    emit_output(b)
    return b


_VOF = np.array([2 * r if r < 16 else 2 * (r - 16) + 1 for r in range(32)])


def host_inputs(inputs):
    hs = np.ascontiguousarray(inputs["L0_attn_a_hidden_states"], np.float32)
    base = dict(
        w_qkvz=np.ascontiguousarray(inputs["L0_attn_a_W_qkvz"], np.float32),
        conv_w=np.ascontiguousarray(inputs["L0_attn_a_conv_w"], np.float32),
        dt_bias=np.ascontiguousarray(
            np.asarray(inputs["L0_attn_a_dt_bias"], np.float32).reshape(HV)[_VOF].reshape(HV, 1)),
        a_log=np.ascontiguousarray(
            np.asarray(inputs["L0_attn_a_A_log"], np.float32).reshape(HV)[_VOF].reshape(HV, 1)),
        norm_w=np.ascontiguousarray(inputs["L0_attn_a_norm_w"], np.float32).reshape(DV, 1),
        w_out=np.ascontiguousarray(inputs["L0_attn_a_W_out"], np.float32),
        time_decay=np.ascontiguousarray(np.asarray(inputs["L0_attn_b_time_decay"], np.float32).reshape(16, 128).T),
        time_first=np.ascontiguousarray(np.asarray(inputs["L0_attn_b_time_first"], np.float32).reshape(16, 128).T),
        w_k=np.ascontiguousarray(inputs["L0_attn_b_Wk"], np.float32),
        w_v=np.ascontiguousarray(inputs["L0_attn_b_Wv"], np.float32),
        w_r=np.ascontiguousarray(inputs["L0_attn_b_Wr"], np.float32),
        w_o=np.ascontiguousarray(inputs["L0_attn_b_Wo"], np.float32),
        w_fk=np.ascontiguousarray(inputs["L0_attn_b_Wfk"], np.float32),
        w_fv=np.ascontiguousarray(inputs["L0_attn_b_Wfv"], np.float32),
        ln1_w=np.ascontiguousarray(np.asarray(inputs["L0_attn_b_ln1_w"], np.float32).reshape(16, 128).T),
        ln1_b=np.ascontiguousarray(np.asarray(inputs["L0_attn_b_ln1_b"], np.float32).reshape(16, 128).T),
        ln2_w=np.ascontiguousarray(np.asarray(inputs["L0_attn_b_ln2_w"], np.float32).reshape(16, 128).T),
        ln2_b=np.ascontiguousarray(np.asarray(inputs["L0_attn_b_ln2_b"], np.float32).reshape(16, 128).T),
        w_ffn=np.ascontiguousarray(np.asarray(inputs["L0_ffn_weight"], np.float32).T),
    )
    w_ba = np.asarray(inputs["L0_attn_a_W_ba"], np.float32).reshape(DM, HK, 4)
    base["w_ba"] = np.ascontiguousarray(
        np.concatenate([w_ba[:, :, 0], w_ba[:, :, 1], w_ba[:, :, 2], w_ba[:, :, 3]], axis=1))
    maps = []
    for c in range(8):
        bb, hf = c // 2, c % 2
        t0 = hf * TLOC
        xs = hs[bb].T
        sl = np.zeros((DM, TX), np.float32)
        if t0 >= 3:
            sl[:, :] = xs[:, t0 - 3:t0 + TLOC]
        else:
            sl[:, 3:] = xs[:, t0:t0 + TLOC]
        m = dict(base)
        m["x_in"] = np.ascontiguousarray(sl)
        m["half_mask"] = np.full((128, 1), float(hf), np.float32)
        m["first_mask"] = np.full((128, 1), float(1 - hf), np.float32)
        maps.append(m)
    return maps


_BUILT = {}


def kernel(**inputs):
    if "nc" not in _BUILT:
        _BUILT["nc"] = build().finalize()
    maps = host_inputs(inputs)
    res = run_bass_kernel_spmd(_BUILT["nc"], maps, core_ids=list(range(8)))
    out = np.zeros((4, 2048, DM), np.float32)
    for c in range(8):
        bb, hf = c // 2, c % 2
        out[bb, hf * TLOC:(hf + 1) * TLOC, :] = res.results[c]["x_out"].T
    return out


# version 03 (final)
# speedup vs baseline: 1.0519x; 1.0519x over previous
import numpy as np
import ml_dtypes

import concourse.bass as bass
import concourse.mybir as mybir
from concourse.bass_utils import run_bass_kernel_spmd
from concourse.bass_isa import ReduceOp

F32 = mybir.dt.float32
BF16 = mybir.dt.bfloat16
FP8 = mybir.dt.float8e4
AF = mybir.ActivationFunctionType
ALU = mybir.AluOpType
AX = mybir.AxisListType

DM = 2048
TLOC = 1024
TX = TLOC + 3
HK, HV, DK, DV = 16, 32, 128, 128
NCH, CL = 8, 128
SOLVE_LEVELS = 3
DFF = 8192

DRAINS = True
ENGINES = ("tensor", "vector", "scalar", "sync", "gpsimd")


def vr(v):
    return (v % 2) * 16 + v // 2


class Res:
    __slots__ = ("name", "writer", "readers")

    def __init__(self, name):
        self.name = name
        self.writer = None
        self.readers = []


class T:
    __slots__ = ("ap", "res")

    def __init__(self, ap, res):
        self.ap = ap
        self.res = res

    def __getitem__(self, idx):
        return T(self.ap[idx], self.res)


class B:
    def __init__(self, dbg=()):
        self.nc = bass.Bass("TRN2", target_bir_lowering=False, num_devices=8,
                            detect_race_conditions=False)
        self.ops = {e: [] for e in ENGINES}
        self.waited = {e: {} for e in ENGINES}
        self.opidx = {e: 0 for e in ENGINES}
        self.last_drain = {e: -1 for e in ENGINES}
        self.dbg = set(dbg)
        self.dbg_outs = {}
        nc = self.nc
        self.sems = {}
        for e in ENGINES:
            self.sems[e] = Ctr(nc.alloc_semaphore("sem_" + e))
        self.dpool = {q: [Ctr(nc.alloc_semaphore(f"dq_{q}_{i}")) for i in range(32)]
                      for q in ("sync", "gpsimd")}
        self.dcnt = {"sync": 0, "gpsimd": 0}
        self.dset = set()
        for q in self.dpool:
            self.dset.update(self.dpool[q])

        self._declare_dram()
        self._declare_consts()
        self._declare_sbuf()

    def on(self, eng, fn):
        self.ops[eng].append(fn)

    def _wait(self, eng, sem, val):
        if val <= 0:
            return
        if self.waited[eng].get(sem, -1) >= val:
            return
        self.waited[eng][sem] = val
        self.on(eng, lambda e, s=sem.h, v=val: e.wait_ge(s, v))

    def emit(self, eng, fn, reads=(), writes=(), dma=None):
        rres = [x.res if isinstance(x, T) else x for x in reads]
        wres = [x.res if isinstance(x, T) else x for x in writes]
        need_drain = False
        for r in rres + wres:
            w = r.writer
            if w is None:
                continue
            weng, wsem, wval, widx = w
            if wsem in self.dset or weng != eng:
                self._wait(eng, wsem, wval)
            elif widx > self.last_drain[eng]:
                need_drain = True
        for r in wres:
            for (reng, rsem, rval, ridx) in r.readers:
                if rsem in self.dset or reng != eng:
                    self._wait(eng, rsem, rval)
                elif ridx > self.last_drain[eng]:
                    need_drain = True
        if need_drain and eng in ("vector", "scalar", "gpsimd") and DRAINS:
            self.on(eng, lambda e: e.drain())
            self.last_drain[eng] = self.opidx[eng]
        if dma is None:
            sem = self.sems[eng]
            sem.n += 1
            val = sem.n
            self.on(eng, lambda e, fn=fn, s=sem.h: fn(e).then_inc(s, 1))
        else:
            pool = self.dpool[dma]
            sem = pool[self.dcnt[dma] % len(pool)]
            self.dcnt[dma] += 1
            sem.n += 16
            val = sem.n
            self.on(eng, lambda e, fn=fn, s=sem.h: fn(e).then_inc(s, 16))
        self.opidx[eng] += 1
        idx = self.opidx[eng]
        for r in rres:
            for j, (re_, rs_, rv_, ri_) in enumerate(r.readers):
                if re_ == eng and rs_ is sem:
                    r.readers[j] = (eng, sem, val, idx)
                    break
            else:
                r.readers.append((eng, sem, val, idx))
        for r in wres:
            r.writer = (eng, sem, val, idx)
            r.readers = []
        return (sem, val)

    def barrier_on(self, eng, marks):
        for (sem, val) in marks:
            self._wait(eng, sem, val)

    def fence(self):
        allsems = list(self.sems.values()) + [c for q in self.dpool for c in self.dpool[q]]
        for eng in ENGINES:
            for sem in allsems:
                if sem is self.sems.get(eng):
                    continue
                self._wait(eng, sem, sem.n)

    def mm(self, out, lhsT, rhs, start=True, stop=True, reads=(), writes=()):
        o = out.ap if isinstance(out, T) else out
        l = lhsT.ap if isinstance(lhsT, T) else lhsT
        r = rhs.ap if isinstance(rhs, T) else rhs
        rd = [x for x in (lhsT, rhs) if isinstance(x, T)] + list(reads)
        wr = ([out] if isinstance(out, T) else []) + list(writes)
        return self.emit("tensor", lambda t: t.matmul(o, l, r, start=start, stop=stop),
                         reads=rd, writes=wr)

    def tr(self, out, in_, f32=False, p=128):
        o = out.ap if isinstance(out, T) else out
        i = in_.ap if isinstance(in_, T) else in_
        ident = self.ident_f32s if f32 else self.ident_bfs
        iap = ident.ap if p == 128 else ident.ap[0:p, 0:p]
        rd = [x for x in (in_,) if isinstance(x, T)]
        wr = [out] if isinstance(out, T) else []
        return self.emit("tensor", lambda t: t.transpose(o, i, iap),
                         reads=rd + [ident], writes=wr)

    def vop(self, fn, reads=(), writes=()):
        return self.emit("vector", fn, reads=reads, writes=writes)

    def sop(self, fn, reads=(), writes=()):
        return self.emit("scalar", fn, reads=reads, writes=writes)

    def gop(self, fn, reads=(), writes=()):
        return self.emit("gpsimd", fn, reads=reads, writes=writes)

    def dma(self, queue, out, in_):
        o = out.ap if isinstance(out, T) else out
        i = in_.ap if isinstance(in_, T) else in_
        rd = [x for x in (in_,) if isinstance(x, T)]
        wr = [out] if isinstance(out, T) else []
        return self.emit(queue, lambda e: e.dma_start(out=o, in_=i),
                         reads=rd, writes=wr, dma=queue)

    def act(self, out, in_, func, bias=0.0, scale=1.0, accum_out=None):
        o = out.ap if isinstance(out, T) else out
        i = in_.ap if isinstance(in_, T) else in_
        ao = accum_out.ap if isinstance(accum_out, T) else accum_out
        wr = [x for x in (out, accum_out) if isinstance(x, T)]
        rd = [x for x in (in_, bias, scale) if isinstance(x, T)]
        bias = bias.ap if isinstance(bias, T) else bias
        scale = scale.ap if isinstance(scale, T) else scale
        if ao is None:
            return self.sop(lambda e: e.activation(out=o, in_=i, func=func, bias=bias, scale=scale),
                            reads=rd, writes=wr)
        return self.sop(lambda e: e.activation(out=o, in_=i, func=func, bias=bias, scale=scale,
                                               accum_out=ao), reads=rd, writes=wr)

    def tt(self, out, in0, in1, op=ALU.mult):
        o, a, c = (x.ap if isinstance(x, T) else x for x in (out, in0, in1))
        rd = [x for x in (in0, in1) if isinstance(x, T)]
        wr = [out] if isinstance(out, T) else []
        return self.vop(lambda e: e.tensor_tensor(out=o, in0=a, in1=c, op=op), reads=rd, writes=wr)

    def ts(self, out, in0, s1, s2=None, op0=ALU.mult, op1=ALU.bypass):
        o, a = (x.ap if isinstance(x, T) else x for x in (out, in0))
        s1a = s1.ap if isinstance(s1, T) else s1
        s2a = s2.ap if isinstance(s2, T) else s2
        rd = [x for x in (in0, s1, s2) if isinstance(x, T)]
        wr = [out] if isinstance(out, T) else []
        if s2a is None and op1 == ALU.bypass:
            return self.vop(lambda e: e.tensor_single_scalar(out=o, in_=a, scalar=s1a, op=op0),
                            reads=rd, writes=wr)
        return self.vop(lambda e: e.tensor_scalar(out=o, in0=a, scalar1=s1a, scalar2=s2a,
                                                  op0=op0, op1=op1), reads=rd, writes=wr)

    def stt(self, out, in0, scalar, in1, op0=ALU.mult, op1=ALU.add):
        o, a, c = (x.ap if isinstance(x, T) else x for x in (out, in0, in1))
        sa = scalar.ap if isinstance(scalar, T) else scalar
        rd = [x for x in (in0, scalar, in1) if isinstance(x, T)]
        wr = [out] if isinstance(out, T) else []
        return self.vop(lambda e: e.scalar_tensor_tensor(out=o, in0=a, scalar=sa, in1=c,
                                                         op0=op0, op1=op1), reads=rd, writes=wr)

    def cp(self, out, in_, eng="vector"):
        o, a = (x.ap if isinstance(x, T) else x for x in (out, in_))
        rd = [x for x in (in_,) if isinstance(x, T)]
        wr = [out] if isinstance(out, T) else []
        if eng == "scalar":
            return self.sop(lambda e: e.copy(out=o, in_=a), reads=rd, writes=wr)
        return self.vop(lambda e: e.tensor_copy(out=o, in_=a), reads=rd, writes=wr)

    def _declare_dram(self):
        nc = self.nc

        def din(name, shape, dt=F32):
            return T(nc.dram_tensor(name, list(shape), dt, kind="ExternalInput").ap(),
                     Res(name))

        self.x_in = din("x_in", [DM, TX])
        self.w_qkvz = din("w_qkvz", [DM, 12288])
        self.w_ba = din("w_ba", [DM, 64])
        self.conv_w = din("conv_w", [8192, 4])
        self.dt_bias = din("dt_bias", [HV, 1])
        self.a_log = din("a_log", [HV, 1])
        self.norm_w = din("norm_w", [DV, 1])
        self.w_out = din("w_out", [HV * DV, DM])
        self.time_decay = din("time_decay", [128, 16])
        self.time_first = din("time_first", [128, 16])
        self.w_k = din("w_k", [DM, DM])
        self.w_v = din("w_v", [DM, DM])
        self.w_r = din("w_r", [DM, DM])
        self.w_o = din("w_o", [DM, DM])
        self.w_fk = din("w_fk", [DM, DFF])
        self.w_fv = din("w_fv", [DFF, DM])
        self.ln1_w = din("ln1_w", [128, 16])
        self.ln1_b = din("ln1_b", [128, 16])
        self.ln2_w = din("ln2_w", [128, 16])
        self.ln2_b = din("ln2_b", [128, 16])
        self.w_ffn = din("w_ffn", [DM, DM])
        self.half_mask = din("half_mask", [128, 1])
        self.first_mask = din("first_mask", [128, 1])
        self.x_out = T(nc.dram_tensor("x_out", [DM, TLOC], F32, kind="ExternalOutput").ap(),
                       Res("x_out"))
        self.RG = [[0, 1], [2, 3], [4, 5], [6, 7]]
        mkint = lambda n, s, d: T(nc.dram_tensor(n, list(s), d, kind="Internal").ap(), Res(n))
        self.agA_in = mkint("agA_in", [HV * DK, DV], BF16)
        self.agA_out = mkint("agA_out", [2 * HV * DK, DV], BF16)
        self.agB_in = mkint("agB_in", [128, 16], F32)
        self.agB_out = mkint("agB_out", [256, 16], F32)
        self.spill = [mkint(f"spill{i}", [128, 4 * 128], BF16) for i in range(HV * NCH)]
        self.y_sp = [mkint(f"y_sp{i}", [128, TLOC], BF16) for i in range(HK * 4)]
        self.z_sp = [mkint(f"z_sp{v}", [128, TLOC], BF16) for v in range(HV)]
        self.rwkv_sp = [mkint(f"rwkv_sp{i}", [128, TLOC], BF16) for i in range(16)]
        self.xres_sp = [mkint(f"xres_sp{i}", [128, TLOC], F32) for i in range(16)]
        self.gated_sp = [mkint(f"gated_sp{v}", [128, TLOC], BF16) for v in range(HV)]
        self.wkv_sp = [mkint(f"wkv_sp{i}", [128, TLOC], BF16) for i in range(16)]
        self.h2_sp = [mkint(f"h2_sp{i}", [128, TLOC], BF16) for i in range(DFF // 128)]
        self.wd_sp = mkint("wd_sp", [DM, DM], BF16)

    def _declare_consts(self):
        nc = self.nc
        it = nc.inline_tensor
        self._const_specs = []

        def const(name, arr, dt):
            dram = it(np.ascontiguousarray(arr), name="cd_" + name).ap()
            sb = nc.alloc_sbuf_tensor("c_" + name, list(arr.shape), dt)
            t = T(sb[:, :] if len(arr.shape) == 2 else sb[:], Res("c_" + name))
            self._const_specs.append((t, dram))
            return t

        bf = ml_dtypes.bfloat16
        self.ident_bfs = const("ident_bf", np.eye(128, dtype=bf), BF16)
        self.ident_f32s = const("ident_f32", np.eye(128, dtype=np.float32), F32)
        self.ones_col_bfs = const("ones_col_bf", np.ones((128, 1), dtype=bf), BF16)
        self.ones_row_f32s = const("ones_row_f32", np.ones((1, 128), dtype=np.float32), F32)
        self.ones_scan = const("ones_scan", np.ones((32, TLOC), dtype=np.float32), F32)
        ms = np.where(np.arange(128)[:, None] < np.arange(128)[None, :], 0.0, -1e6)
        mi = np.where(np.arange(128)[:, None] <= np.arange(128)[None, :], 0.0, -1e6)
        self.maskb_strict = const("maskb_s", ms.astype(bf), BF16)
        self.maskb_incl = const("maskb_i", mi.astype(bf), BF16)
        sel = np.zeros((32, 32 * 128), dtype=bf)
        for r in range(32):
            sel[r, r * 128:(r + 1) * 128] = 1.0
        self.selb32 = const("selb32", sel, BF16)
        eps = np.zeros((128, 6), dtype=np.float32)
        eps[:, 0] = 1e-5
        eps[:, 1] = 1e-6
        eps[:, 2] = 1.0
        eps[:, 3] = 1.0 / 128.0
        eps[:, 4] = -32.0 * float(np.log(2.0))
        self.epsc = const("epsc", eps, F32)
        self.selb16 = T(self.selb32.ap, self.selb32.res)

    def load_consts(self):
        for (t, dram) in self._const_specs:
            self.dma("sync", t, dram)

    def _declare_sbuf(self):
        nc = self.nc
        self._tcount = 0

        def sbuf(name, shape, dt):
            ap = nc.alloc_sbuf_tensor(name, list(shape), dt)
            return T(ap[:, :], Res(name))

        self.sbuf = sbuf
        self.X = [sbuf(f"X{i}", [128, TX], F32) for i in range(16)]
        self.POOLT = nc.alloc_sbuf_tensor("POOL", [128, 27648], BF16)
        self._pool_res = {}
        self.PF32 = [sbuf(f"PF32_{j}", [128, 1056], F32) for j in range(3)]
        self.WBUF = [sbuf(f"WBUF{j}", [128, 2048], BF16) for j in range(2)]
        self.BAB = sbuf("BAB", [32, TX], F32)
        self.BAA = sbuf("BAA", [32, TX], F32)
        self.G32 = T(self.PF32[0].ap[0:32, 0:TLOC], self.PF32[0].res)
        self.GLNB = T(self.PF32[2].ap[0:32, 0:TLOC], self.PF32[2].res)
        self.SCR32 = T(self.PF32[1].ap[0:32, 0:TLOC], self.PF32[1].res)
        self.SQQ2 = sbuf("SQQ2", [16, TLOC], F32)
        self.SQK2 = sbuf("SQK2", [16, TLOC], F32)
        self.REBA = sbuf("REBA", [32, CL], F32)
        self.REBB = sbuf("REBB", [32, CL], F32)
        self.REBC = sbuf("REBC", [32, CL], F32)
        self.STK = sbuf("STK", [128, CL], F32)
        self.COLS = sbuf("COLS", [128, CL], F32)
        self.EARG0 = sbuf("EARG0", [128, 128], F32)
        self.ECOL = sbuf("ECOL", [128, 128], F32)
        self.CTRIG = [sbuf(f"CTRIG{j}", [32, CL], BF16) for j in range(3)]
        self.CTRIB = [sbuf(f"CTRIB{j}", [32, CL], BF16) for j in range(3)]
        self.CTRI2Q = [sbuf(f"CTRI2Q_{j}", [16, CL], BF16) for j in range(3)]
        self.CTRI2K = [sbuf(f"CTRI2K_{j}", [16, CL], BF16) for j in range(3)]
        self.TRIF = [sbuf(f"TRIF{j}", [32, CL], F32) for j in range(2)]
        self.TRIF2 = [sbuf(f"TRIF2_{j}", [32, CL], F32) for j in range(2)]
        self.KTM = [sbuf("KTM0", [128, 128], BF16)] * 2
        self.VTM = [sbuf("VTM0", [128, 256], BF16)] * 2
        self.KNF = [sbuf("KNF0", [128, 128], BF16)] * 2
        self.QNF = [sbuf("QNF0", [128, 128], BF16)] * 2
        self.QHF = [sbuf(f"QHF{j}", [128, 128], BF16) for j in range(2)]
        self.KKT = [sbuf("KKT0", [128, 128], BF16)] * 2
        self.KQT = [sbuf("KQT0", [128, 128], BF16)] * 2
        self.EB = [sbuf(f"EB{j}", [128, 128], BF16) for j in range(2)] * 2
        self.YLD = [sbuf("YLD0", [128, 512], BF16)] * 2
        self.ARG = [sbuf(f"ARG{j}", [128, 128], F32) for j in range(2)]
        self.EARGB = [sbuf(f"EARGB{j}", [128, 128], BF16) for j in range(2)]
        self.MTP = [sbuf(f"MTP{j}", [128, 128], BF16) for j in range(4)]
        self.MXP = [sbuf(f"MXP{j}", [128, 128], BF16) for j in range(2)]
        self.WOTT = [sbuf(f"WOTT{j}", [128, 128], BF16) for j in range(2)]
        self.RSOL = [sbuf(f"RSOL{j}", [128, 256], BF16) for j in range(4)]
        self.BKD = [sbuf(f"BKD{j}", [128, 128], BF16) for j in range(2)]
        self.BVK = [sbuf(f"BVK{j}", [128, 256], BF16) for j in range(2)]
        self.EFUW = [sbuf(f"EFUW{j}", [128, 512], BF16) for j in range(2)]
        self.DCREP = sbuf("DCREP", [128, HV * NCH], F32)
        self.DCROW = sbuf("DCROW", [32, NCH], F32)
        self.DCROWB = sbuf("DCROWB", [32, NCH], BF16)
        self.STAT = T(self.PF32[0].ap[0:1, :], self.PF32[0].res)
        self.ROW = T(self.PF32[1].ap[0:1, :], self.PF32[1].res)
        self.ROW2 = T(self.PF32[2].ap[0:1, :], self.PF32[2].res)
        self.RSTDB = sbuf("RSTDB", [128, TX], F32)
        self.NMRB = sbuf("NMRB", [128, TX], F32)
        self.CVEC = sbuf("CVEC", [128, 24], F32)
        self.WCOL = sbuf("WCOL", [128, 4], F32)
        self.DIAG = [sbuf(f"DIAG{j}", [128, 128], BF16) for j in range(4)]
        self.XQ8 = [sbuf("XQ8_0", [128, 1024], FP8)] * 2
        self.CVECB = sbuf("CVECB", [128, 128], F32)
        self.HR2 = [sbuf(f"HR2_{j}", [128, 512], BF16) for j in range(2)]
        self.VC0 = sbuf("VC0", [128, 16], F32)
        self.AGST = sbuf("AGST", [128, 16], F32)
        self.PSB = [nc.alloc_psum_tensor(f"PS{i}", [128, 512], F32) for i in range(7)]
        self.PBFB = nc.alloc_psum_tensor("PBF", [128, 1024], BF16)
        self.PRES = [Res(f"psbank{i}") for i in range(8)]
        self.PQ = []
        for i in range(7):
            for q in range(4):
                self.PQ.append(T(self.PSB[i][:, q * 128:(q + 1) * 128], self.PRES[i]))
        self.PH = [T(self.PSB[i][:, 0:256], self.PRES[i]) for i in range(7)] + \
                  [T(self.PSB[i][:, 256:512], self.PRES[i]) for i in range(7)]
        self.PFUL = [T(self.PSB[i][:, 0:512], self.PRES[i]) for i in range(7)]
        self.PB8 = [T(self.PBFB[:, j * 128:(j + 1) * 128], self.PRES[7]) for j in range(8)]

    def pb(self, off, n, tag=None):
        key = (off, n) if tag is None else tag
        if key not in self._pool_res:
            self._pool_res[key] = Res(f"pool@{off}+{n}")
        return T(self.POOLT[:, off:off + n], self._pool_res[key])

    def dbg_tap(self, tag, shape, dt=F32):
        if tag in self.dbg:
            t = T(self.nc.dram_tensor("dbg_" + tag, list(shape), dt,
                                      kind="ExternalOutput").ap(), Res("dbg_" + tag))
            self.dbg_outs[tag] = t
            return t
        return None

    def finalize(self):
        nc = self.nc
        with nc.Block() as block:
            @block.tensor
            def _(t):
                for fn in self.ops["tensor"]:
                    fn(t)

            @block.vector
            def _(e):
                for fn in self.ops["vector"]:
                    fn(e)

            @block.scalar
            def _(e):
                for fn in self.ops["scalar"]:
                    fn(e)

            @block.sync
            def _(e):
                for fn in self.ops["sync"]:
                    fn(e)

            @block.gpsimd
            def _(e):
                for fn in self.ops["gpsimd"]:
                    fn(e)
        return nc


class Ctr:
    def __init__(self, h):
        self.h = h
        self.n = 0



XLN_OFF = 0
MIX_OFF = 16896
Y_OFF = 21120
ZR_OFF = 25344


def rr(t, pat, **kw):
    return T(t.ap.rearrange(pat, **kw), t.res)


def sel16(b, h):
    return T(b.selb32.ap[0:16, h * 128:(h + 1) * 128], b.selb32.res)


def emit_ln(b, c0=0, ncols=TX):
    XB = [b.pb(MIX_OFF + j * 1056, 1056, tag=("mix", j)) for j in range(2)]
    SQ = [b.pb(MIX_OFF + (2 + j) * 1056, 1056, tag=("mix", 2 + j)) for j in range(2)]
    nch = [(0, 512), (512, 512)]
    if ncols > 1024:
        nch.append((1024, ncols - 1024))
    for i in range(16):
        b.cp(XB[i % 2][:, 0:ncols], b.X[i][:, c0:c0 + ncols])
        b.act(SQ[i % 2][:, 0:ncols], b.X[i][:, c0:c0 + ncols], AF.Square)
        for ci, (c0_, w) in enumerate(nch):
            b.mm(T(b.PSB[ci][0:1, 0:w], b.PFUL[ci].res), b.ones_col_bfs,
                 XB[i % 2][:, c0_:c0_ + w], start=(i == 0), stop=(i == 15))
            b.mm(T(b.PSB[3 + ci][0:1, 0:w], b.PFUL[3 + ci].res), b.ones_col_bfs,
                 SQ[i % 2][:, c0_:c0_ + w], start=(i == 0), stop=(i == 15))
    for ci, (c0_, w) in enumerate(nch):
        b.cp(b.STAT[0:1, c0_:c0_ + w], T(b.PSB[ci][0:1, 0:w], b.PFUL[ci].res))
        b.cp(b.ROW2[0:1, c0_:c0_ + w], T(b.PSB[3 + ci][0:1, 0:w], b.PFUL[3 + ci].res))
    b.ts(b.STAT[0:1, 0:ncols], b.STAT[0:1, 0:ncols], 1.0 / DM)
    b.ts(b.ROW2[0:1, 0:ncols], b.ROW2[0:1, 0:ncols], 1.0 / DM)
    b.act(b.ROW[0:1, 0:ncols], b.STAT[0:1, 0:ncols], AF.Square)
    b.tt(b.ROW2[0:1, 0:ncols], b.ROW2[0:1, 0:ncols], b.ROW[0:1, 0:ncols], op=ALU.subtract)
    b.act(b.ROW[0:1, 0:ncols], b.ROW2[0:1, 0:ncols], AF.Ln,
          bias=T(b.epsc.ap[0:1, 0:1], b.epsc.res))
    b.act(b.ROW2[0:1, 0:ncols], b.ROW[0:1, 0:ncols], AF.Exp, scale=-0.5)
    b.tt(b.STAT[0:1, 0:ncols], b.STAT[0:1, 0:ncols], b.ROW2[0:1, 0:ncols])
    b.ts(b.STAT[0:1, 0:ncols], b.STAT[0:1, 0:ncols], -1.0)
    for ci, (c0_, w) in enumerate(nch):
        b.mm(T(b.PSB[0][:, 0:w], b.PFUL[0].res), b.ones_row_f32s,
             b.ROW2[0:1, c0_:c0_ + w])
        b.mm(T(b.PSB[1][:, 0:w], b.PFUL[1].res), b.ones_row_f32s,
             b.STAT[0:1, c0_:c0_ + w])
        b.cp(b.RSTDB[:, c0_:c0_ + w], T(b.PSB[0][:, 0:w], b.PFUL[0].res))
        b.cp(b.NMRB[:, c0_:c0_ + w], T(b.PSB[1][:, 0:w], b.PFUL[1].res))


def emit_xln(b, c0=0, ncols=TX, aff=None):
    XLN = [b.pb(XLN_OFF + i * 1056, 1056, tag=("xln", i)) for i in range(16)]
    ch = [(0, min(1024, ncols))]
    if ncols > 1024:
        ch.append((1024, ncols - 1024))
    for i in range(16):
        xf = T(b.PF32[i % 2].ap[:, 0:1024], b.PF32[i % 2].res)
        for (q0, qw) in ch:
            b.tt(xf[:, 0:qw], b.X[i][:, c0 + q0:c0 + q0 + qw], b.RSTDB[:, q0:q0 + qw])
            if aff is None:
                b.tt(XLN[i][:, q0:q0 + qw], xf[:, 0:qw], b.NMRB[:, q0:q0 + qw], op=ALU.add)
            else:
                wcols, bcols = aff
                b.tt(xf[:, 0:qw], xf[:, 0:qw], b.NMRB[:, q0:q0 + qw], op=ALU.add)
                b.ts(XLN[i][:, q0:q0 + qw], xf[:, 0:qw],
                     T(wcols.ap[:, i:i + 1], wcols.res),
                     T(bcols.ap[:, i:i + 1], bcols.res), op0=ALU.mult, op1=ALU.add)
    return XLN


def emit_attn_a(b):
    emit_ln(b, c0=0, ncols=TX)
    dbgS = b.dbg_tap("lnstats", [128, 2 * TX], F32)
    if dbgS is not None:
        b.dma("sync", dbgS[:, 0:TX], b.RSTDB[:, 0:TX])
        b.dma("sync", dbgS[:, TX:2 * TX], b.NMRB[:, 0:TX])
    XLN = emit_xln(b, c0=0, ncols=TX)
    MIXT = [b.pb(MIX_OFF + j * 1056, 1056, tag=("mix", j)) for j in range(4)]
    YT = [b.pb(Y_OFF + j * 1056, 1056, tag=("yt", j)) for j in range(4)]
    ZROT = [b.pb(ZR_OFF + j * 1056, 1056, tag=("zr", j)) for j in range(2)]

    b.dma("gpsimd", rr(b.WBUF[0][:, 0:1024], "p (k o) -> p k o", k=16),
          rr(b.w_ba, "(k p) o -> p k o", p=128))
    for ci, (c0, w) in enumerate([(0, 512), (512, 512), (1024, 3)]):
        ptb = T(b.PSB[4][0:32, 0:w], b.PFUL[4].res)
        pta = T(b.PSB[5][0:32, 0:w], b.PFUL[5].res)
        for kt in range(16):
            b.mm(ptb, b.WBUF[0][:, kt * 64:kt * 64 + 32], XLN[kt][:, c0:c0 + w],
                 start=(kt == 0), stop=(kt == 15))
            b.mm(pta, b.WBUF[0][:, kt * 64 + 32:kt * 64 + 64], XLN[kt][:, c0:c0 + w],
                 start=(kt == 0), stop=(kt == 15))
        b.cp(T(b.BAB.ap[:, c0:c0 + w], b.BAB.res), ptb)
        b.cp(T(b.BAA.ap[:, c0:c0 + w], b.BAA.res), pta)

    dbgB = b.dbg_tap("bab", [64, TX], F32)
    if dbgB is not None:
        b.dma("sync", dbgB[0:32, :], b.BAB)
        b.dma("sync", dbgB[32:64, :], b.BAA)
    bar = T(b.BAB.ap[:, 3:3 + TLOC], b.BAB.res)
    aar = T(b.BAA.ap[:, 3:3 + TLOC], b.BAA.res)
    b.act(b.SCR32, bar, AF.Abs)
    b.act(b.SCR32, b.SCR32, AF.Exp, scale=-1.0)
    b.act(b.SCR32, b.SCR32, AF.Ln, bias=T(b.epsc.ap[0:32, 2:3], b.epsc.res))
    b.ts(b.GLNB, bar, -1.0, 0.0, op0=ALU.mult, op1=ALU.max)
    b.tt(b.GLNB, b.GLNB, b.SCR32, op=ALU.add)
    b.ts(b.GLNB, b.GLNB, -1.0)
    b.dma("sync", T(b.CVEC.ap[0:HV, 0:1], b.CVEC.res), b.dt_bias)
    b.dma("sync", T(b.CVEC.ap[0:HV, 1:2], b.CVEC.res), b.a_log)
    b.dma("sync", T(b.CVEC.ap[0:DV, 2:3], b.CVEC.res), b.norm_w)
    b.dma("sync", T(b.CVEC.ap[:, 3:4], b.CVEC.res), b.half_mask)
    b.dma("sync", T(b.CVEC.ap[:, 4:5], b.CVEC.res), b.first_mask)
    b.act(T(b.CVEC.ap[0:HV, 5:6], b.CVEC.res), T(b.CVEC.ap[0:HV, 1:2], b.CVEC.res), AF.Exp)
    b.ts(T(b.CVEC.ap[0:HV, 5:6], b.CVEC.res), T(b.CVEC.ap[0:HV, 5:6], b.CVEC.res), -1.0)
    b.ts(b.SCR32, aar, T(b.CVEC.ap[0:HV, 0:1], b.CVEC.res), op0=ALU.add)
    b.act(b.G32, b.SCR32, AF.Abs)
    b.act(b.G32, b.G32, AF.Exp, scale=-1.0)
    b.act(b.G32, b.G32, AF.Ln, bias=T(b.epsc.ap[0:32, 2:3], b.epsc.res))
    b.ts(b.SCR32, b.SCR32, 0.0, op0=ALU.max)
    b.tt(b.SCR32, b.SCR32, b.G32, op=ALU.add)
    b.ts(b.SCR32, b.SCR32, T(b.CVEC.ap[0:HV, 5:6], b.CVEC.res), op0=ALU.mult)
    b.vop(lambda e: e.tensor_tensor_scan(out=b.G32.ap, data0=b.ones_scan.ap,
                                         data1=b.SCR32.ap, initial=0.0,
                                         op0=ALU.mult, op1=ALU.add),
          reads=[b.ones_scan, b.SCR32], writes=[b.G32])
    b.tt(b.GLNB, b.GLNB, b.G32, op=ALU.add)
    for ci in range(NCH):
        ce = (ci + 1) * CL
        dst = T(b.DCROW.ap[:, ci:ci + 1], b.DCROW.res)
        if ci == 0:
            b.cp(dst, T(b.G32.ap[:, ce - 1:ce], b.G32.res))
        else:
            b.ts(dst, T(b.G32.ap[:, ce - 1:ce], b.G32.res),
                 T(b.G32.ap[:, ci * CL - 1:ci * CL], b.G32.res), op0=ALU.subtract)
    b.act(b.DCROW, b.DCROW, AF.Exp)
    b.cp(b.DCROWB, b.DCROW)
    for r in range(32):
        pq = b.PQ[24 + (r % 2)]
        b.mm(T(pq.ap[:, 0:NCH], pq.res),
             T(b.selb32.ap[:, r * 128:(r + 1) * 128], b.selb32.res), b.DCROWB)
        b.cp(T(b.DCREP.ap[:, r * NCH:(r + 1) * NCH], b.DCREP.res), T(pq.ap[:, 0:NCH], pq.res))

    dbg = b.dbg_tap("rows", [32, 2 * TLOC])
    if dbg is not None:
        b.dma("sync", dbg[0:32, 0:TLOC], b.G32)
        b.dma("sync", dbg[0:32, TLOC:2 * TLOC], b.GLNB)

    NGRID = [(0, 512, 0), (512, 512, 1), (1024, 3, 2)]
    for h in range(HK):
        for pas in range(2):
            c_lo = 768 * h + 384 * pas
            for qt in range(4):
                k0, nk = qt * 4, 4
                b.dma("gpsimd",
                      rr(T(b.WBUF[qt % 2].ap[:, 0:nk * 384], b.WBUF[qt % 2].res),
                         "p (k o) -> p k o", k=nk),
                      rr(T(b.w_qkvz.ap[128 * k0:128 * (k0 + nk), c_lo:c_lo + 384],
                           b.w_qkvz.res), "(k p) o -> p k o", p=128))
                for ktl in range(nk):
                    kt = k0 + ktl
                    for o in range(3):
                        lhs = b.WBUF[qt % 2][:, ktl * 384 + o * 128: ktl * 384 + (o + 1) * 128]
                        for (c0, w, ni) in NGRID[:2]:
                            ps = T(b.PSB[o * 2 + ni][:, 0:w], b.PFUL[o * 2 + ni].res)
                            b.mm(ps, lhs, XLN[kt][:, c0:c0 + w],
                                 start=(kt == 0), stop=(kt == 15))
                    tail_ps = T(b.PSB[6][:, 0:384], b.PFUL[6].res)
                    b.mm(tail_ps, XLN[kt][:, 899:1027],
                         b.WBUF[qt % 2][:, ktl * 384:(ktl + 1) * 384],
                         start=(kt == 0), stop=(kt == 15))
            tstg = T(b.YLD[0].ap[:, 0:384], b.YLD[0].res)
            b.cp(tstg, T(b.PSB[6][:, 0:384], b.PFUL[6].res))
            for o in range(3):
                gidx = pas * 3 + o
                ptr = T(b.PB8[0].ap[:, 125:128], b.PB8[0].res)
                b.tr(T(b.PB8[0].ap[:, 0:128], b.PB8[0].res),
                     T(b.YLD[0].ap[:, o * 128:(o + 1) * 128], b.YLD[0].res))
                for (c0, w, ni) in NGRID[:2]:
                    src = T(b.PSB[o * 2 + ni][:, 0:w], b.PFUL[o * 2 + ni].res)
                    if gidx < 4:
                        b.cp(MIXT[gidx][:, c0:c0 + w], src)
                    else:
                        b.act(ZROT[gidx - 4][:, c0:c0 + w], src, AF.Silu)
                if gidx < 4:
                    b.cp(MIXT[gidx][:, 1024:1027], ptr)
                else:
                    b.act(ZROT[gidx - 4][:, 1024:1027], ptr, AF.Silu)
                if gidx >= 4:
                    v = 2 * h + (gidx - 4)
                    b.dma("gpsimd", b.z_sp[v], ZROT[gidx - 4][:, 3:3 + TLOC])
            conv_tiles = [0, 1, 2] if pas == 0 else [3]
            for gidx in conv_tiles:
                if gidx == 0:
                    r0 = 128 * h
                elif gidx == 1:
                    r0 = 2048 + 128 * h
                else:
                    r0 = 4096 + 256 * h + 128 * (gidx - 2)
                b.dma("sync", b.WCOL, T(b.conv_w.ap[r0:r0 + 128, :], b.conv_w.res))
                for j in range(4):
                    b.ts(b.DIAG[j], b.ident_bfs, T(b.WCOL.ap[:, j:j + 1], b.WCOL.res),
                         op0=ALU.mult)
                for nn in range(2):
                    ps = T(b.PSB[nn][:, 0:512], b.PFUL[nn].res)
                    for j in range(4):
                        b.mm(ps, b.DIAG[j], MIXT[gidx][:, 512 * nn + j: 512 * nn + j + 512],
                             start=(j == 0), stop=(j == 3))
                    b.act(YT[gidx][:, 3 + 512 * nn: 3 + 512 * (nn + 1)], ps, AF.Silu)
                b.dma("gpsimd", b.y_sp[h * 4 + gidx], YT[gidx][:, 3:3 + TLOC])
                if gidx in (0, 1):
                    b.act(MIXT[gidx][:, 0:TLOC], YT[gidx][:, 3:3 + TLOC], AF.Square)
                    for nn in range(2):
                        ps = T(b.PSB[2 + nn][0:1, 0:512], b.PFUL[2 + nn].res)
                        b.mm(ps, b.ones_col_bfs, MIXT[gidx][:, nn * 512:(nn + 1) * 512])
                        b.cp(b.ROW[0:1, nn * 512:(nn + 1) * 512], ps)
                    dst = b.SQQ2 if gidx == 0 else b.SQK2
                    b.dma("sync", T(dst.ap[h:h + 1, 0:TLOC], dst.res),
                          b.ROW[0:1, 0:TLOC])

    dbgY = b.dbg_tap("ysp", [6 * 128, TLOC], BF16)
    if dbgY is not None:
        for g in range(4):
            b.dma("sync", dbgY[g * 128:(g + 1) * 128, :], b.y_sp[g])
        for v in range(2):
            b.dma("sync", dbgY[(4 + v) * 128:(5 + v) * 128, :], b.z_sp[v])

    b.act(b.SQQ2, b.SQQ2, AF.Ln, bias=T(b.epsc.ap[0:16, 1:2], b.epsc.res))
    b.ts(b.SQQ2, b.SQQ2, -0.5, -0.5 * float(np.log(128.0)), op0=ALU.mult, op1=ALU.add)
    b.act(b.SQK2, b.SQK2, AF.Ln, bias=T(b.epsc.ap[0:16, 1:2], b.epsc.res))
    b.ts(b.SQK2, b.SQK2, -0.5)

    b.fence()
    SLOC = [b.pb(vr_ * 128, 128, tag=("sloc", vr_)) for vr_ in range(HV)]
    for v_ in range(HV):
        b.vop(lambda e, v_=v_: e.memset(SLOC[v_].ap, 0.0), writes=[SLOC[v_]])
    for ci in range(NCH):
        cs = ci * CL
        if ci == 0:
            b.cp(b.REBA, T(b.G32.ap[:, cs:cs + CL], b.G32.res))
            b.cp(b.REBB, T(b.GLNB.ap[:, cs:cs + CL], b.GLNB.res))
        else:
            gcs = T(b.G32.ap[:, cs - 1:cs], b.G32.res)
            b.ts(b.REBA, T(b.G32.ap[:, cs:cs + CL], b.G32.res), gcs, op0=ALU.subtract)
            b.ts(b.REBB, T(b.GLNB.ap[:, cs:cs + CL], b.GLNB.res), gcs, op0=ALU.subtract)
        gce = T(b.G32.ap[:, cs + CL - 1:cs + CL], b.G32.res)
        b.ts(b.REBC, T(b.G32.ap[:, cs:cs + CL], b.G32.res),
             gce, -1.0, op0=ALU.subtract, op1=ALU.mult)
        b.dma("sync", T(b.STK.ap[0:32, :], b.STK.res), b.REBA)
        b.dma("sync", T(b.STK.ap[32:64, :], b.STK.res), b.REBB)
        b.dma("sync", T(b.STK.ap[64:96, :], b.STK.res), b.REBC)
        b.dma("sync", T(b.STK.ap[96:112, :], b.STK.res),
              T(b.SQQ2.ap[:, cs:cs + CL], b.SQQ2.res))
        b.dma("sync", T(b.STK.ap[112:128, :], b.STK.res),
              T(b.SQK2.ap[:, cs:cs + CL], b.SQK2.res))
        pq = b.PQ[24]
        b.tr(pq, b.STK, f32=True)
        b.cp(b.COLS, pq)
        b.tt(T(b.EARG0.ap[:, 0:16], b.EARG0.res), T(b.COLS.ap[:, 0:16], b.COLS.res),
             T(b.COLS.ap[:, 96:112], b.COLS.res), op=ALU.add)
        b.tt(T(b.EARG0.ap[:, 16:32], b.EARG0.res), T(b.COLS.ap[:, 16:32], b.COLS.res),
             T(b.COLS.ap[:, 96:112], b.COLS.res), op=ALU.add)
        b.cp(T(b.EARG0.ap[:, 32:96], b.EARG0.res), T(b.COLS.ap[:, 32:96], b.COLS.res))
        b.tt(T(b.EARG0.ap[:, 96:128], b.EARG0.res), T(b.COLS.ap[:, 32:64], b.COLS.res),
             T(b.COLS.ap[:, 0:32], b.COLS.res), op=ALU.subtract)
        b.act(b.ECOL, b.EARG0, AF.Exp)
        for (src, dsts) in ((b.REBA, b.CTRIG), (b.REBB, b.CTRIB)):
            b.cp(dsts[0], src)
            b.tt(b.TRIF[0], src, dsts[0], op=ALU.subtract)
            b.cp(dsts[1], b.TRIF[0])
            b.tt(b.TRIF[1], b.TRIF[0], dsts[1], op=ALU.subtract)
            b.cp(dsts[2], b.TRIF[1])
        tf0 = T(b.TRIF2[0].ap[0:16, :], b.TRIF2[0].res)
        tf1 = T(b.TRIF2[1].ap[0:16, :], b.TRIF2[1].res)
        for (srcx, dsts) in ((T(b.SQQ2.ap[:, cs:cs + CL], b.SQQ2.res), b.CTRI2Q),
                             (T(b.SQK2.ap[:, cs:cs + CL], b.SQK2.res), b.CTRI2K)):
            b.cp(dsts[0], srcx)
            b.tt(tf0, srcx, dsts[0], op=ALU.subtract)
            b.cp(dsts[1], tf0)
            b.tt(tf1, tf0, dsts[1], op=ALU.subtract)
            b.cp(dsts[2], tf1)

        for h in range(HK):
            par = h % 2
            YL = b.YLD[0]
            for g in range(4):
                b.dma("sync", YL[:, g * 128:(g + 1) * 128],
                      T(b.y_sp[h * 4 + g].ap[:, cs:cs + CL], b.y_sp[h * 4 + g].res))
            pq0 = b.PQ[0]
            for lvl in range(3):
                b.mm(pq0, sel16(b, h), b.CTRI2K[lvl], start=(lvl == 0), stop=(lvl == 2))
            b.act(b.EB[0], pq0, AF.Exp)
            pq1 = b.PQ[1]
            for lvl in range(3):
                b.mm(pq1, sel16(b, h), b.CTRI2Q[lvl], start=(lvl == 0), stop=(lvl == 2))
            b.act(b.EB[1], pq1, AF.Exp)
            b.tt(b.KNF[0], YL[:, 128:256], b.EB[0])
            b.tt(b.QNF[0], YL[:, 0:128], b.EB[1])
            pb0 = b.PB8[0]
            b.tr(pb0, b.KNF[0])
            b.cp(b.KTM[0], pb0)
            pb1 = b.PB8[1]
            b.tr(pb1, YL[:, 256:384])
            b.cp(T(b.VTM[0].ap[:, 0:128], b.VTM[0].res), pb1)
            pb2 = b.PB8[2]
            b.tr(pb2, YL[:, 384:512])
            b.cp(T(b.VTM[0].ap[:, 128:256], b.VTM[0].res), pb2)
            pq2 = b.PQ[2]
            b.mm(pq2, b.KNF[0], b.KNF[0])
            b.ts(b.KKT[0], pq2, -1.0)
            pq3 = b.PQ[3]
            b.mm(pq3, b.KNF[0], b.QNF[0])
            b.cp(b.KQT[0], pq3)
            for j in range(2):
                vhr = h + 16 * j
                ecol = lambda c0_, vhr=vhr: T(b.ECOL.ap[:, c0_ + vhr:c0_ + vhr + 1], b.ECOL.res)
                gcol = T(b.COLS.ap[:, vhr:vhr + 1], b.COLS.res)
                b.ts(T(b.BVK[j].ap[:, 0:128], b.BVK[j].res),
                     T(b.VTM[0].ap[:, j * 128:(j + 1) * 128], b.VTM[0].res), ecol(96))
                b.ts(T(b.BVK[j].ap[:, 128:256], b.BVK[j].res), b.KTM[0], ecol(32))
                b.ts(b.BKD[j], b.KTM[0], ecol(64))
                qb = (2 + j) * 4
                pqa = b.PQ[qb + 0]
                for lvl in range(3):
                    b.mm(pqa, T(b.selb32.ap[:, vhr * 128:(vhr + 1) * 128], b.selb32.res),
                         b.CTRIG[lvl],
                         start=(lvl == 0), stop=False)
                for lvl in range(3):
                    b.mm(pqa, sel16(b, h), b.CTRI2Q[lvl],
                         start=False, stop=(lvl == 2))
                b.act(b.ARG[j], pqa, AF.Exp)
                b.tt(b.QHF[j], YL[:, 0:128], b.ARG[j])
                pqb = b.PQ[qb + 1]
                for lvl in range(3):
                    b.mm(pqb, T(b.selb32.ap[:, vhr * 128:(vhr + 1) * 128], b.selb32.res),
                         b.CTRIB[lvl],
                         start=(lvl == 0), stop=False)
                b.mm(pqb, b.ident_bfs, b.maskb_strict, start=False, stop=True)
                b.ts(b.ARG[j], pqb, gcol, op0=ALU.subtract)
                b.act(b.EARGB[j], b.ARG[j], AF.Exp)
                b.tt(b.MTP[2 * j], b.EARGB[j], b.KKT[0])
                pqc = b.PQ[qb + 2]
                for lvl in range(3):
                    b.mm(pqc, T(b.selb32.ap[:, vhr * 128:(vhr + 1) * 128], b.selb32.res),
                         b.CTRIG[lvl],
                         start=(lvl == 0), stop=False)
                b.mm(pqc, b.ident_bfs, b.maskb_incl, start=False, stop=True)
                b.ts(b.ARG[j], pqc, gcol, op0=ALU.subtract)
                b.act(b.EARGB[j], b.ARG[j], AF.Exp)
                b.tt(b.WOTT[j], b.EARGB[j], b.KQT[0])
                rsrc = b.BVK[j]
                mt = b.MTP[2 * j]
                for lvl in range(SOLVE_LEVELS):
                    ph = b.PH[13]
                    b.mm(ph, mt, rsrc, start=True, stop=False)
                    b.mm(ph, b.ident_bfs, rsrc, start=False, stop=True)
                    rdst = b.RSOL[2 * j + (lvl % 2)]
                    b.cp(rdst, ph)
                    if lvl < SOLVE_LEVELS - 1:
                        pbx = b.PB8[3]
                        b.tr(pbx, mt)
                        b.cp(b.MXP[j], pbx)
                        pqd = b.PQ[qb + 3]
                        b.mm(pqd, b.MXP[j], mt)
                        mtn = b.MTP[2 * j + 1] if mt is b.MTP[2 * j] else b.MTP[2 * j]
                        b.cp(mtn, pqd)
                        mt = mtn
                    rsrc = rdst
                R = rsrc
                RP = T(R.ap[:, 0:128], R.res)
                RZ = T(R.ap[:, 128:256], R.res)
                pqe = b.PQ[qb + 0]
                b.mm(pqe, RZ, b.BKD[j])
                b.ts(T(b.EFUW[j].ap[:, 0:128], b.EFUW[j].res), pqe, -1.0)
                pqf = b.PQ[qb + 1]
                b.mm(pqf, b.BKD[j], RP)
                b.cp(T(b.EFUW[j].ap[:, 128:256], b.EFUW[j].res), pqf)
                pqg = b.PQ[qb + 2]
                b.mm(pqg, RZ, b.WOTT[j])
                b.tt(T(b.EFUW[j].ap[:, 256:384], b.EFUW[j].res), b.QHF[j], pqg,
                     op=ALU.subtract)
                pqh = b.PQ[qb + 3]
                b.mm(pqh, RP, b.WOTT[j])
                b.cp(T(b.EFUW[j].ap[:, 384:512], b.EFUW[j].res), pqh)
                psc = b.PQ[25]
                b.mm(psc, T(b.EFUW[j].ap[:, 0:128], b.EFUW[j].res), SLOC[vhr],
                     start=True, stop=False)
                b.mm(psc, b.ident_bfs, T(b.EFUW[j].ap[:, 128:256], b.EFUW[j].res),
                     start=False, stop=True)
                b.stt(SLOC[vhr], SLOC[vhr],
                      T(b.DCREP.ap[:, vhr * NCH + ci:vhr * NCH + ci + 1], b.DCREP.res),
                      psc, op0=ALU.mult, op1=ALU.add)
                b.dma("gpsimd", b.spill[vhr * NCH + ci], b.EFUW[j])

    b.fence()
    for v_ in range(HV):
        b.dma("sync", T(b.agA_in.ap[vr(v_) * 128:(vr(v_) + 1) * 128, :], b.agA_in.res),
              SLOC[vr(v_)])
    b.gop(lambda e: e.collective_compute(
        "AllGather", ALU.bypass, replica_groups=b.RG,
        ins=[b.agA_in.ap[:, :]], outs=[b.agA_out.ap[:, :]]),
        reads=[b.agA_in], writes=[b.agA_out])
    for v_ in range(HV):
        r_ = vr(v_)
        stg = b.YLD[0]
        b.dma("sync", stg[:, 0:128],
              T(b.agA_out.ap[r_ * 128:(r_ + 1) * 128, :], b.agA_out.res))
        b.ts(SLOC[r_], stg[:, 0:128], T(b.CVEC.ap[:, 3:4], b.CVEC.res), op0=ALU.mult)

    b.fence()
    ZP = [b.pb(MIX_OFF + 2 * 1056 + j * 128, 128, tag=("zp", j)) for j in range(2)]
    GTO = [b.pb(MIX_OFF + 2 * 1056 + 256 + j * 128, 128, tag=("gto", j)) for j in range(2)]
    LEFU = b.EFUW
    GTS = b.EARGB
    SQG = b.MXP
    for v_ in range(HV):
        r_ = vr(v_)
        for ci in range(NCH):
            par = ci % 2
            b.dma("gpsimd", LEFU[par], b.spill[r_ * NCH + ci])
            b.dma("sync", ZP[par], T(b.z_sp[v_].ap[:, ci * CL:(ci + 1) * CL], b.z_sp[v_].res))
            pq = b.PQ[par * 4 + 0]
            b.mm(pq, SLOC[r_], T(LEFU[par].ap[:, 256:384], LEFU[par].res),
                 start=True, stop=False)
            b.mm(pq, b.ident_bfs, T(LEFU[par].ap[:, 384:512], LEFU[par].res),
                 start=False, stop=True)
            b.tt(GTS[par], pq, ZP[par])
            b.act(SQG[par], GTS[par], AF.Square)
            pr = b.PQ[par * 4 + 1]
            b.mm(T(pr.ap[0:1, :], pr.res), b.ones_col_bfs, SQG[par])
            b.act(T(b.ROW2.ap[0:1, 0:128], b.ROW2.res), T(pr.ap[0:1, :], pr.res),
                  AF.Ln, bias=T(b.epsc.ap[0:1, 1:2], b.epsc.res),
                  scale=T(b.epsc.ap[0:1, 3:4], b.epsc.res))
            b.act(T(b.ROW2.ap[0:1, 128:256], b.ROW2.res),
                  T(b.ROW2.ap[0:1, 0:128], b.ROW2.res), AF.Exp, scale=-0.5)
            pbc = b.PQ[par * 4 + 2]
            b.mm(pbc, b.ones_row_f32s, T(b.ROW2.ap[0:1, 128:256], b.ROW2.res))
            b.stt(GTO[par], pbc, T(b.CVEC.ap[:, 2:3], b.CVEC.res), GTS[par],
                  op0=ALU.mult, op1=ALU.mult)
            b.dma("sync", T(b.gated_sp[v_].ap[:, ci * CL:(ci + 1) * CL], b.gated_sp[v_].res),
                  GTO[par])
            psc = b.PQ[25]
            b.mm(psc, T(LEFU[par].ap[:, 0:128], LEFU[par].res), SLOC[r_],
                 start=True, stop=False)
            b.mm(psc, b.ident_bfs, T(LEFU[par].ap[:, 128:256], LEFU[par].res),
                 start=False, stop=True)
            b.stt(SLOC[r_], SLOC[r_],
                  T(b.DCREP.ap[:, r_ * NCH + ci:r_ * NCH + ci + 1], b.DCREP.res),
                  psc, op0=ALU.mult, op1=ALU.add)

    dbgG = b.dbg_tap("gated", [HV * 128, TLOC], BF16)
    if dbgG is not None:
        for v_ in range(HV):
            b.dma("sync", dbgG[v_ * 128:(v_ + 1) * 128, :], b.gated_sp[v_])

    b.fence()
    GBUF = [b.pb(XLN_OFF + 4096 + j * 512, 512, tag=("gbuf", j)) for j in range(2)]
    OGS = [(0, 7), (7, 7), (14, 2)]
    for (o0, no) in OGS:
        for th in range(2):
            t0 = th * 512
            for kt in range(32):
                b.dma("sync", GBUF[kt % 2],
                      T(b.gated_sp[kt].ap[:, t0:t0 + 512], b.gated_sp[kt].res))
                b.dma("gpsimd",
                      rr(T(b.WBUF[kt % 2].ap[:, 0:no * 128], b.WBUF[kt % 2].res),
                         "p (o c) -> p o c", o=no),
                      rr(T(b.w_out.ap[kt * 128:(kt + 1) * 128,
                           o0 * 128:(o0 + no) * 128], b.w_out.res),
                         "p (o c) -> p o c", o=no))
                for o in range(no):
                    b.mm(T(b.PSB[o][:, 0:512], b.PFUL[o].res),
                         b.WBUF[kt % 2][:, o * 128:(o + 1) * 128], GBUF[kt % 2],
                         start=(kt == 0), stop=(kt == 31))
            for o in range(no):
                oc = o0 + o
                b.tt(b.X[oc][:, 3 + t0:3 + t0 + 512], b.X[oc][:, 3 + t0:3 + t0 + 512],
                     T(b.PSB[o][:, 0:512], b.PFUL[o].res), op=ALU.add)

    dbgX = b.dbg_tap("x_attnA", [DM, TLOC], F32)
    if dbgX is not None:
        for i in range(16):
            b.dma("sync", dbgX[i * 128:(i + 1) * 128, :], b.X[i][:, 3:3 + TLOC])

def emit_ffn(b, first):
    b.fence()
    emit_ln(b, c0=3, ncols=TLOC)
    XDB = [b.pb(i * 1024, 1024, tag=("xdb", i)) for i in range(16)]
    OCAST = [b.pb(16896 + j * 1024, 1024, tag=("oc8", j)) for j in range(2)]
    WDB = b.pb(19000, 2048, tag="wdb")
    XTF = b.ARG[0]
    SCC = b.ECOL
    if first:
        for kt in range(16):
            for hh in range(2):
                wf = T(b.PF32[hh].ap[:, 0:1024], b.PF32[hh].res)
                b.dma("sync", wf, T(b.w_ffn.ap[kt * 128:(kt + 1) * 128,
                                               hh * 1024:(hh + 1) * 1024], b.w_ffn.res))
                sq = T(b.PF32[2].ap[:, 0:1024], b.PF32[2].res)
                b.act(sq, wf, AF.Square)
                b.vop(lambda e, sq=sq: e.tensor_reduce(
                    out=b.CVECB.ap[:, 0:8],
                    in_=sq.ap.rearrange("p (c n) -> p c n", c=8),
                    axis=AX.X, op=ALU.max), reads=[b.PF32[2]], writes=[b.CVECB])
                pq = b.PQ[24]
                b.tr(T(pq.ap[0:8, 0:128], pq.res), T(b.CVECB.ap[:, 0:8], b.CVECB.res), f32=True)
                b.cp(T(b.ECOL.ap[0:8, 0:128], b.ECOL.res), T(pq.ap[0:8, 0:128], pq.res))
                b.vop(lambda e: e.tensor_reduce(
                    out=b.ECOL.ap[0:8, 126:127], in_=b.ECOL.ap[0:8, 0:128],
                    axis=AX.X, op=ALU.max), reads=[b.ECOL], writes=[b.ECOL])
                mc = T(b.ECOL.ap[0:8, 126:127], b.ECOL.res)
                b.ts(mc, mc, 1e-8, op0=ALU.max)
                b.act(mc, mc, AF.Ln)
                b.act(mc, mc, AF.Exp, scale=-0.5)
                b.tr(T(pq.ap[0:1, 8:16], pq.res), mc, f32=True, p=8)
                b.cp(T(b.ECOL.ap[0:1, 16:24], b.ECOL.res), T(pq.ap[0:1, 8:16], pq.res))
                pb2 = b.PQ[25]
                b.mm(T(pb2.ap[:, 0:8], pb2.res), b.ones_row_f32s,
                     T(b.ECOL.ap[0:1, 16:24], b.ECOL.res))
                b.cp(T(b.ECOL.ap[:, 24:32], b.ECOL.res), T(pb2.ap[:, 0:8], pb2.res))
                inv = T(b.ECOL.ap[:, 32:40], b.ECOL.res)
                b.vop(lambda e: e.reciprocal(out=b.ECOL.ap[:, 32:40],
                                             in_=b.ECOL.ap[:, 24:32]),
                      reads=[b.ECOL], writes=[b.ECOL])
                b.ts(T(b.ECOL.ap[:, 24:32], b.ECOL.res),
                     T(b.ECOL.ap[:, 24:32], b.ECOL.res), 224.0)
                b.ts(inv, inv, 1.0 / 224.0)
                for blk in range(8):
                    b.ts(T(b.XQ8[0].ap[:, blk * 128:(blk + 1) * 128], b.XQ8[0].res),
                         T(wf.ap[:, blk * 128:(blk + 1) * 128], wf.res),
                         T(b.ECOL.ap[:, 24 + blk:25 + blk], b.ECOL.res))
                    b.ts(T(WDB.ap[:, hh * 1024 + blk * 128:hh * 1024 + (blk + 1) * 128],
                           WDB.res),
                         T(b.XQ8[0].ap[:, blk * 128:(blk + 1) * 128], b.XQ8[0].res),
                         T(b.ECOL.ap[:, 32 + blk:33 + blk], b.ECOL.res))
            b.dma("gpsimd", T(b.wd_sp.ap[kt * 128:(kt + 1) * 128, :], b.wd_sp.res), WDB)
    for i in range(16):
        xf = T(b.PF32[0].ap[:, 0:TLOC], b.PF32[0].res)
        b.tt(xf, b.X[i][:, 3:3 + TLOC], b.RSTDB[:, 0:TLOC])
        b.tt(xf, xf, b.NMRB[:, 0:TLOC], op=ALU.add)
        for blk in range(8):
            pq = b.PQ[24]
            b.tr(pq, T(xf.ap[:, blk * 128:(blk + 1) * 128], b.PF32[0].res), f32=True)
            b.cp(XTF, pq)
            b.act(b.EARGB[0], XTF, AF.Square)
            b.vop(lambda e: e.tensor_reduce(
                out=SCC.ap[:, 0:1], in_=b.EARGB[0].ap, axis=AX.X, op=ALU.max),
                reads=[b.EARGB[0]], writes=[SCC])
            sc = T(SCC.ap[:, 0:1], SCC.res)
            b.ts(sc, sc, 1e-8, op0=ALU.max)
            b.act(sc, sc, AF.Ln)
            b.act(sc, sc, AF.Exp, scale=-0.5)
            b.ts(sc, sc, 224.0)
            b.ts(T(b.XQ8[0].ap[:, 0:128], b.XQ8[0].res), XTF, sc)
            b.vop(lambda e: e.reciprocal(out=SCC.ap[:, 1:2], in_=SCC.ap[:, 0:1]),
                  reads=[SCC], writes=[SCC])
            b.ts(b.EARGB[0], T(b.XQ8[0].ap[:, 0:128], b.XQ8[0].res),
                 T(SCC.ap[:, 1:2], SCC.res))
            pb2 = T(b.PB8[0].ap, b.PB8[0].res)
            b.tr(pb2, b.EARGB[0])
            b.cp(XDB[i][:, blk * 128:(blk + 1) * 128], pb2)
    for oc in range(16):
        b.dma("sync", rr(T(b.WBUF[oc % 2].ap[:, 0:2048], b.WBUF[oc % 2].res),
                         "p (k o) -> p k o", k=16),
              rr(T(b.wd_sp.ap[:, oc * 128:(oc + 1) * 128], b.wd_sp.res),
                 "(k p) o -> p k o", p=128))
        for nn in range(2):
            ps = T(b.PSB[4 + oc % 2][:, 0:512], b.PFUL[4 + oc % 2].res)
            for kt in range(16):
                b.mm(ps, b.WBUF[oc % 2][:, kt * 128:(kt + 1) * 128],
                     XDB[kt][:, nn * 512:(nn + 1) * 512],
                     start=(kt == 0), stop=(kt == 15))
            b.cp(OCAST[oc % 2][:, nn * 512:(nn + 1) * 512], ps)
        b.tt(b.X[oc][:, 3:3 + TLOC], b.X[oc][:, 3:3 + TLOC], OCAST[oc % 2], op=ALU.add)
    dbgX = b.dbg_tap("x_ffn1" if first else "x_ffn2", [DM, TLOC], F32)
    if dbgX is not None:
        for i in range(16):
            b.dma("sync", dbgX[i * 128:(i + 1) * 128, :], b.X[i][:, 3:3 + TLOC])


def emit_attn_b(b):
    b.fence()
    CB = b.CVECB
    for (cix, src) in ((0, b.ln1_w), (1, b.ln1_b), (2, b.ln2_w), (3, b.ln2_b),
                       (4, b.time_decay), (5, b.time_first)):
        b.dma("sync", T(CB.ap[:, 32 + cix * 16:32 + (cix + 1) * 16], CB.res), src)
    b.act(T(CB.ap[:, 96:112], CB.res), T(CB.ap[:, 96:112], CB.res), AF.Exp)
    b.ts(T(CB.ap[:, 96:112], CB.res), T(CB.ap[:, 96:112], CB.res), -1.0)
    b.act(T(CB.ap[:, 0:16], CB.res), T(CB.ap[:, 96:112], CB.res), AF.Exp)
    b.act(T(CB.ap[:, 16:32], CB.res), T(CB.ap[:, 112:128], CB.res), AF.Exp)
    b.ts(T(CB.ap[:, 16:32], CB.res), T(CB.ap[:, 16:32], CB.res), float(2.0 ** -32))

    emit_ln(b, c0=3, ncols=TLOC)
    XL = emit_xln(b, c0=3, ncols=TLOC,
                  aff=(T(CB.ap[:, 32:48], CB.res), T(CB.ap[:, 48:64], CB.res)))
    for i in range(16):
        b.dma("gpsimd", b.xres_sp[i], b.X[i][:, 3:3 + TLOC])
    for i in range(16):
        xf = T(b.PF32[i % 2].ap[:, 0:TLOC], b.PF32[i % 2].res)
        b.tt(xf, b.X[i][:, 3:3 + TLOC], b.RSTDB[:, 0:TLOC])
        b.tt(b.X[i][:, 3:3 + TLOC], xf, b.NMRB[:, 0:TLOC], op=ALU.add)
    EKR = [b.pb(16896 + j * 1024, 1024, tag=("abr", j)) for j in range(2)]
    UR = [b.pb(16896 + (2 + j) * 1024, 1024, tag=("abr", 2 + j)) for j in range(2)]
    WKR = [b.pb(16896 + (4 + j) * 1024, 1024, tag=("abr", 4 + j)) for j in range(2)]
    SGR = [b.pb(16896 + (6 + j) * 1024, 1024, tag=("abr", 6 + j)) for j in range(2)]
    RMP = T(b.PF32[2].ap[:, 0:TLOC], b.PF32[2].res)
    SC1 = T(b.PF32[1].ap[:, 0:TLOC], b.PF32[1].res)
    b.vop(lambda e: e.memset(SC1.ap, 1.0), writes=[b.PF32[1]])
    b.vop(lambda e: e.tensor_tensor_scan(out=RMP.ap, data0=SC1.ap,
                                         data1=SC1.ap, initial=0.0,
                                         op0=ALU.mult, op1=ALU.add),
          reads=[b.PF32[1]], writes=[b.PF32[2]])

    def proj(w_dram, oc, evac, pbank):
        b.dma("gpsimd", rr(T(b.WBUF[oc % 2].ap[:, 0:2048], b.WBUF[oc % 2].res),
                           "p (k o) -> p k o", k=16),
              rr(T(w_dram.ap[:, oc * 128:(oc + 1) * 128], w_dram.res),
                 "(k p) o -> p k o", p=128))
        for nn in range(2):
            ps = T(b.PSB[pbank + oc % 2][:, 0:512], b.PFUL[pbank + oc % 2].res)
            for kt in range(16):
                b.mm(ps, b.WBUF[oc % 2][:, kt * 128:(kt + 1) * 128],
                     XL[kt][:, nn * 512:(nn + 1) * 512],
                     start=(kt == 0), stop=(kt == 15))
            evac(nn, ps)

    for oc in range(16):
        par = oc % 2
        proj(b.w_k, oc,
             lambda nn, ps, par=par: b.act(EKR[par][:, nn * 512:(nn + 1) * 512], ps, AF.Exp,
                                           bias=T(b.epsc.ap[:, 4:5], b.epsc.res)), 0)

        def vev(nn, ps, oc=oc, par=par):
            b.tt(UR[par][:, nn * 512:(nn + 1) * 512],
                 EKR[par][:, nn * 512:(nn + 1) * 512], ps)
            if nn == 0:
                b.cp(T(b.VC0.ap[:, oc:oc + 1], b.VC0.res), T(ps.ap[:, 0:1], ps.res))
        proj(b.w_v, oc, vev, 2)
        dcol = T(CB.ap[:, oc:oc + 1], CB.res)
        b.ts(SC1, RMP, 0.0, dcol, op0=ALU.mult, op1=ALU.add)
        b.vop(lambda e, oc=oc, par=par: e.tensor_tensor_scan(
            out=WKR[par].ap, data0=SC1.ap, data1=UR[par].ap,
            initial=0.0, op0=ALU.mult, op1=ALU.add),
            reads=[b.PF32[1], UR[par]], writes=[WKR[par]])
        b.cp(T(b.AGST.ap[:, oc:oc + 1], b.AGST.res),
             T(WKR[par].ap[:, 1023:1024], WKR[par].res))
        b.dma("gpsimd", b.wkv_sp[oc], WKR[par])
    b.dma("sync", b.agB_in, b.AGST)
    b.gop(lambda e: e.collective_compute(
        "AllGather", ALU.bypass, replica_groups=b.RG,
        ins=[b.agB_in.ap[:, :]], outs=[b.agB_out.ap[:, :]]),
        reads=[b.agB_in], writes=[b.agB_out])
    b.dma("sync", b.AGST, T(b.agB_out.ap[0:128, :], b.agB_out.res))
    b.ts(b.AGST, b.AGST, T(b.CVEC.ap[:, 3:4], b.CVEC.res), op0=ALU.mult)
    for oc in range(16):
        par = oc % 2
        b.dma("sync", WKR[par], b.wkv_sp[oc])
        b.ts(SC1, RMP, T(CB.ap[:, 96 + oc:97 + oc], CB.res), op0=ALU.mult)
        b.act(SC1, SC1, AF.Exp)
        b.stt(WKR[par], SC1, T(b.AGST.ap[:, oc:oc + 1], b.AGST.res), WKR[par],
              op0=ALU.mult, op1=ALU.add)
        t1 = T(b.PF32[1].ap[:, 0:1], b.PF32[1].res)
        b.tt(t1, T(CB.ap[:, 16 + oc:17 + oc], CB.res), T(b.VC0.ap[:, oc:oc + 1], b.VC0.res))
        b.tt(t1, t1, T(b.CVEC.ap[:, 4:5], b.CVEC.res))
        b.tt(T(WKR[par].ap[:, 0:1], WKR[par].res), T(WKR[par].ap[:, 0:1], WKR[par].res),
             t1, op=ALU.add)

        def rev(nn, ps, oc=oc, par=par):
            b.act(SGR[par][:, nn * 512:(nn + 1) * 512], ps, AF.Sigmoid)
            b.stt(EKR[par][:, nn * 512:(nn + 1) * 512],
                  SGR[par][:, nn * 512:(nn + 1) * 512], float(2.0 ** 32),
                  WKR[par][:, nn * 512:(nn + 1) * 512], op0=ALU.mult, op1=ALU.mult)
        proj(b.w_r, oc, rev, 4)
        b.dma("gpsimd", b.rwkv_sp[oc], EKR[par])
    b.fence()
    GBUF = [b.pb(16896 + j * 512, 512, tag=("gb2", j)) for j in range(2)]
    for (o0, no) in [(0, 7), (7, 7), (14, 2)]:
        for th in range(2):
            t0 = th * 512
            for kt in range(16):
                b.dma("sync", GBUF[kt % 2],
                      T(b.rwkv_sp[kt].ap[:, t0:t0 + 512], b.rwkv_sp[kt].res))
                b.dma("gpsimd",
                      rr(T(b.WBUF[kt % 2].ap[:, 0:no * 128], b.WBUF[kt % 2].res),
                         "p (o c) -> p o c", o=no),
                      rr(T(b.w_o.ap[kt * 128:(kt + 1) * 128,
                           o0 * 128:(o0 + no) * 128], b.w_o.res),
                         "p (o c) -> p o c", o=no))
                for o in range(no):
                    b.mm(T(b.PSB[o][:, 0:512], b.PFUL[o].res),
                         b.WBUF[kt % 2][:, o * 128:(o + 1) * 128], GBUF[kt % 2],
                         start=(kt == 0), stop=(kt == 15))
            for o in range(no):
                oc = o0 + o
                b.tt(b.X[oc][:, 3 + t0:3 + t0 + 512], b.X[oc][:, 3 + t0:3 + t0 + 512],
                     T(b.PSB[o][:, 0:512], b.PFUL[o].res), op=ALU.add)

    b.fence()
    emit_ln(b, c0=3, ncols=TLOC)
    XL2 = emit_xln(b, c0=3, ncols=TLOC,
                   aff=(T(CB.ap[:, 64:80], CB.res), T(CB.ap[:, 80:96], CB.res)))
    for oc in range(DFF // 128):
        b.dma("gpsimd", rr(T(b.WBUF[oc % 2].ap[:, 0:2048], b.WBUF[oc % 2].res),
                           "p (k o) -> p k o", k=16),
              rr(T(b.w_fk.ap[:, oc * 128:(oc + 1) * 128], b.w_fk.res),
                 "(k p) o -> p k o", p=128))
        for nn in range(2):
            ps = T(b.PSB[oc % 4][:, 0:512], b.PFUL[oc % 4].res)
            for kt in range(16):
                b.mm(ps, b.WBUF[oc % 2][:, kt * 128:(kt + 1) * 128],
                     XL2[kt][:, nn * 512:(nn + 1) * 512],
                     start=(kt == 0), stop=(kt == 15))
            b.act(b.HR2[oc % 2], ps, AF.Relu)
            b.tt(T(b.pb(16896 + (oc % 2) * 512, 512, tag=("h2r", oc % 2)).ap, 
                   b.pb(16896 + (oc % 2) * 512, 512, tag=("h2r", oc % 2)).res),
                 b.HR2[oc % 2], b.HR2[oc % 2])
            b.dma("gpsimd", T(b.h2_sp[oc].ap[:, nn * 512:(nn + 1) * 512], b.h2_sp[oc].res),
                  b.pb(16896 + (oc % 2) * 512, 512, tag=("h2r", oc % 2)))
    b.fence()
    H2B = [b.pb(16896 + j * 512, 512, tag=("h2b", j)) for j in range(2)]
    for (o0, no) in [(0, 7), (7, 7), (14, 2)]:
        for th in range(2):
            t0 = th * 512
            for kt in range(DFF // 128):
                b.dma("sync", H2B[kt % 2],
                      T(b.h2_sp[kt].ap[:, t0:t0 + 512], b.h2_sp[kt].res))
                b.dma("gpsimd",
                      rr(T(b.WBUF[kt % 2].ap[:, 0:no * 128], b.WBUF[kt % 2].res),
                         "p (o c) -> p o c", o=no),
                      rr(T(b.w_fv.ap[kt * 128:(kt + 1) * 128,
                           o0 * 128:(o0 + no) * 128], b.w_fv.res),
                         "p (o c) -> p o c", o=no))
                for o in range(no):
                    b.mm(T(b.PSB[o][:, 0:512], b.PFUL[o].res),
                         b.WBUF[kt % 2][:, o * 128:(o + 1) * 128], H2B[kt % 2],
                         start=(kt == 0), stop=(kt == 63))
            for o in range(no):
                oc = o0 + o
                b.tt(b.X[oc][:, 3 + t0:3 + t0 + 512], b.X[oc][:, 3 + t0:3 + t0 + 512],
                     T(b.PSB[o][:, 0:512], b.PFUL[o].res), op=ALU.add)
    for i in range(16):
        xf = T(b.PF32[i % 2].ap[:, 0:TLOC], b.PF32[i % 2].res)
        b.dma("sync", xf, b.xres_sp[i])
        b.tt(b.X[i][:, 3:3 + TLOC], b.X[i][:, 3:3 + TLOC], xf, op=ALU.add)
    dbgX = b.dbg_tap("x_attnB", [DM, TLOC], F32)
    if dbgX is not None:
        for i in range(16):
            b.dma("sync", dbgX[i * 128:(i + 1) * 128, :], b.X[i][:, 3:3 + TLOC])
    dbgW = b.dbg_tap("wkv", [DM, TLOC], BF16)
    if dbgW is not None:
        for i in range(16):
            b.dma("sync", dbgW[i * 128:(i + 1) * 128, :], b.wkv_sp[i])


def emit_output(b):
    b.fence()
    marks = []
    for i in range(16):
        marks.append(b.dma("sync", T(b.x_out.ap[i * 128:(i + 1) * 128, :], b.x_out.res),
                           b.X[i][:, 3:3 + TLOC]))
    for eng in ENGINES:
        for (sem, val) in marks:
            b._wait(eng, sem, val)


def build(dbg=(), phases=("attn_a", "ffn1", "attn_b", "ffn2")):
    b = B(dbg)
    b.load_consts()
    for i in range(16):
        b.dma("sync", b.X[i], T(b.x_in.ap[128 * i:128 * (i + 1), :], b.x_in.res))
    if "attn_a" in phases:
        emit_attn_a(b)
    if "ffn1" in phases:
        emit_ffn(b, first=True)
    if "attn_b" in phases:
        emit_attn_b(b)
    if "ffn2" in phases:
        emit_ffn(b, first="ffn1" not in phases)
    emit_output(b)
    return b


_VOF = np.array([2 * r if r < 16 else 2 * (r - 16) + 1 for r in range(32)])


def host_inputs(inputs):
    hs = np.ascontiguousarray(inputs["L0_attn_a_hidden_states"], np.float32)
    base = dict(
        w_qkvz=np.ascontiguousarray(inputs["L0_attn_a_W_qkvz"], np.float32),
        conv_w=np.ascontiguousarray(inputs["L0_attn_a_conv_w"], np.float32),
        dt_bias=np.ascontiguousarray(
            np.asarray(inputs["L0_attn_a_dt_bias"], np.float32).reshape(HV)[_VOF].reshape(HV, 1)),
        a_log=np.ascontiguousarray(
            np.asarray(inputs["L0_attn_a_A_log"], np.float32).reshape(HV)[_VOF].reshape(HV, 1)),
        norm_w=np.ascontiguousarray(inputs["L0_attn_a_norm_w"], np.float32).reshape(DV, 1),
        w_out=np.ascontiguousarray(inputs["L0_attn_a_W_out"], np.float32),
        time_decay=np.ascontiguousarray(np.asarray(inputs["L0_attn_b_time_decay"], np.float32).reshape(16, 128).T),
        time_first=np.ascontiguousarray(np.asarray(inputs["L0_attn_b_time_first"], np.float32).reshape(16, 128).T),
        w_k=np.ascontiguousarray(inputs["L0_attn_b_Wk"], np.float32),
        w_v=np.ascontiguousarray(inputs["L0_attn_b_Wv"], np.float32),
        w_r=np.ascontiguousarray(inputs["L0_attn_b_Wr"], np.float32),
        w_o=np.ascontiguousarray(inputs["L0_attn_b_Wo"], np.float32),
        w_fk=np.ascontiguousarray(inputs["L0_attn_b_Wfk"], np.float32),
        w_fv=np.ascontiguousarray(inputs["L0_attn_b_Wfv"], np.float32),
        ln1_w=np.ascontiguousarray(np.asarray(inputs["L0_attn_b_ln1_w"], np.float32).reshape(16, 128).T),
        ln1_b=np.ascontiguousarray(np.asarray(inputs["L0_attn_b_ln1_b"], np.float32).reshape(16, 128).T),
        ln2_w=np.ascontiguousarray(np.asarray(inputs["L0_attn_b_ln2_w"], np.float32).reshape(16, 128).T),
        ln2_b=np.ascontiguousarray(np.asarray(inputs["L0_attn_b_ln2_b"], np.float32).reshape(16, 128).T),
        w_ffn=np.ascontiguousarray(np.asarray(inputs["L0_ffn_weight"], np.float32).T),
    )
    w_ba = np.asarray(inputs["L0_attn_a_W_ba"], np.float32).reshape(DM, HK, 4)
    base["w_ba"] = np.ascontiguousarray(
        np.concatenate([w_ba[:, :, 0], w_ba[:, :, 1], w_ba[:, :, 2], w_ba[:, :, 3]], axis=1))
    maps = []
    for c in range(8):
        bb, hf = c // 2, c % 2
        t0 = hf * TLOC
        xs = hs[bb].T
        sl = np.zeros((DM, TX), np.float32)
        if t0 >= 3:
            sl[:, :] = xs[:, t0 - 3:t0 + TLOC]
        else:
            sl[:, 3:] = xs[:, t0:t0 + TLOC]
        m = dict(base)
        m["x_in"] = np.ascontiguousarray(sl)
        m["half_mask"] = np.full((128, 1), float(hf), np.float32)
        m["first_mask"] = np.full((128, 1), float(1 - hf), np.float32)
        maps.append(m)
    return maps


_BUILT = {}


def kernel(**inputs):
    if "nc" not in _BUILT:
        _BUILT["nc"] = build().finalize()
    maps = host_inputs(inputs)
    res = run_bass_kernel_spmd(_BUILT["nc"], maps, core_ids=list(range(8)))
    out = np.zeros((4, 2048, DM), np.float32)
    for c in range(8):
        bb, hf = c // 2, c % 2
        out[bb, hf * TLOC:(hf + 1) * TLOC, :] = res.results[c]["x_out"].T
    return out
